# version 05
# baseline (speedup 1.0000x reference)
import numpy as np
import ml_dtypes
import concourse.bass as bass
import concourse.mybir as mybir
from concourse.bass_utils import run_bass_kernel_spmd

F32 = mybir.dt.float32
BF16 = mybir.dt.bfloat16
AF = mybir.ActivationFunctionType
ALU = mybir.AluOpType
AX = mybir.AxisListType

DM = 4096
L = 2048
DI = 8192
NH = 128
HD = 64
NG = 8
DS = 128
CONVK = 4
CDIM = DI + 2 * NG * DS
DIN = 2 * DI + 2 * NG * DS + NH
HF = 11008
EPS = 1e-5

NKC = DM // 128
NT = L // 128
NOT_IN = DIN // 128
NCT = CDIM // 128
NXT = DI // 128
NCH = L // 128
NOT_F1 = HF // 128
W2P = 5504
NKC_W2 = W2P // 128
NKC_H = HF // 128
NWO = DI // 128

TAPS = ()
FFN1_LORA = True
SSD_BARRIER = False
FFN1_PREPASS = True
FFN1_NOT = NOT_F1


def _t(x):
    return np.ascontiguousarray(x)


def prep_weights(inp):
    w = {}
    w["WinT4"] = _t(inp["L0_attn_in_proj_w"].reshape(NOT_IN, 128, NKC, 128)
                    .transpose(0, 3, 2, 1))
    w["conv_w"] = _t(inp["L0_attn_conv_w"].reshape(NCT, 128, CONVK)
                     .transpose(1, 0, 2).reshape(128, NCT * CONVK))
    w["conv_b"] = _t(inp["L0_attn_conv_b"].reshape(NCT, 128).T)
    w["dt_bias"] = _t(inp["L0_attn_dt_bias"].reshape(128, 1))
    w["A_log"] = _t(inp["L0_attn_A_log"].reshape(128, 1))
    w["Drep"] = _t(np.repeat(inp["L0_attn_D"], HD).reshape(NXT, 128).T)
    w["normw"] = _t(inp["L0_attn_norm_w"].reshape(NWO, 128).T)
    w["WoutT"] = _t(inp["L0_attn_out_proj_w"].T)
    w["W1T4"] = _t(inp["L0_ffn_w1q"].reshape(NOT_F1, 128, 8, 128).transpose(0, 3, 2, 1))
    w["W3T4"] = _t(inp["L0_ffn_w3q"].reshape(NOT_F1, 128, 8, 128).transpose(0, 3, 2, 1))
    w2t = inp["L0_ffn_w2q"].T
    w["W2T2"] = _t(np.concatenate([w2t, w2t], axis=0))
    w["a1T"] = _t(inp["L0_ffn_a1"].T.reshape(NKC, 128, 2)
                  .transpose(1, 0, 2).reshape(128, NKC * 2))
    w["b1T"] = _t(inp["L0_ffn_b1"].T)
    w["a3T"] = _t(inp["L0_ffn_a3"].T.reshape(NKC, 128, 2)
                  .transpose(1, 0, 2).reshape(128, NKC * 2))
    w["b3T"] = _t(inp["L0_ffn_b3"].T)
    w["a2T"] = _t(inp["L0_ffn_a2"].T.reshape(NKC_H, 128, 2)
                  .transpose(1, 0, 2).reshape(128, NKC_H * 2))
    w["b2T"] = _t(inp["L0_ffn_b2"].T)
    return w


SEM_NAMES = ["wL", "aL", "tL", "sS", "pc", "pS",
             "sT", "sV", "sA", "sG", "bar"]


class Prog:

    def __init__(self, nc, block, sems):
        self.nc = nc
        self.block = block
        self.sem = sems
        self.cnt = {k: 0 for k in sems}
        self.nbar = 0

    def inc(self, ins, name, n=1):
        ins.then_inc(self.sem[name], n)
        self.cnt[name] += n
        return self.cnt[name]

    def bump(self, name, n):
        self.cnt[name] += n

    def snap(self):
        return dict(self.cnt)

    def barrier(self):
        snap = self.snap()
        self.nbar += 1
        target = 5 * self.nbar
        for sec in (self.block.tensor, self.block.vector, self.block.scalar,
                    self.block.gpsimd, self.block.sync):
            def body(eng, snap=snap, target=target):
                for k, v in snap.items():
                    if k != "bar" and v > 0:
                        eng.wait_ge(self.sem[k], v)
                eng.sem_inc(self.sem["bar"], 1)
                eng.wait_ge(self.sem["bar"], target)
            sec(body)
        self.cnt["bar"] = target



def phase_const(nc, P, block, bufs, ins):
    sems = P.sem
    (ident_bf, ident_f32, ones_bf, ones_f32, zeros128, m0mask, vec_dtb, vec_nA,
     m_convb, m_convw, m_Drep, m_normw, eps_col, one_col) = bufs
    dt_bias, A_log, conv_b, conv_w, Drep, normw = ins

    @block.gpsimd
    def _(g):
        g.dma_start(out=vec_dtb[:], in_=dt_bias).then_inc(sems["aL"], 16)
        g.dma_start(out=vec_nA[:], in_=A_log).then_inc(sems["aL"], 16)
        g.dma_start(out=m_convb[:], in_=conv_b).then_inc(sems["aL"], 16)
        g.dma_start(out=m_convw[:], in_=conv_w).then_inc(sems["aL"], 16)
        g.dma_start(out=m_Drep[:], in_=Drep).then_inc(sems["aL"], 16)
        g.dma_start(out=m_normw[:], in_=normw).then_inc(sems["aL"], 16)
    P.bump("aL", 6 * 16)

    @block.vector
    def _(v):
        v.memset(zeros128[:], 0.0)
        v.memset(ones_bf[:], 1.0)
        v.memset(ones_f32[:], 1.0)
        v.memset(eps_col[:], EPS)
        v.memset(one_col[:], 1.0)
        P.inc(v.memset(ident_f32[:], 1.0), "sV")

    @block.gpsimd
    def _(g):
        g.wait_ge(sems["sV"], P.cnt["sV"])
        g.affine_select(out=ident_f32[:], in_=ident_f32[:], pattern=[[1, 128]],
                        compare_op=ALU.is_equal, fill=0.0, base=0,
                        channel_multiplier=-1)
        P.inc(g.affine_select(out=m0mask[:], in_=zeros128[:], pattern=[[1, 128]],
                              compare_op=ALU.is_ge, fill=-1e30, base=0,
                              channel_multiplier=-1), "sG")

    @block.vector
    def _(v):
        v.wait_ge(sems["sG"], P.cnt["sG"])
        P.inc(v.tensor_copy(ident_bf[:], ident_f32[:]), "sV")

    @block.scalar
    def _(s):
        s.wait_ge(sems["aL"], P.cnt["aL"])
        P.inc(s.activation(out=vec_nA[:], in_=vec_nA[:], func=AF.Exp, scale=1.0), "sA")

    @block.vector
    def _(v):
        v.wait_ge(sems["sA"], P.cnt["sA"])
        P.inc(v.tensor_scalar_mul(vec_nA[:], vec_nA[:], -1.0), "sV")


def phase_ln(nc, P, block, x_in, XN_TM, eps_col, bufs):
    sems = P.sem
    xbuf, sqbuf, xnbuf, st = bufs
    if True:
        base = P.snap()

        @block.sync
        def _(sy):
            for i in range(NT):
                sl = (i % 2) * DM
                if i >= 2:
                    sy.wait_ge(sems["sV"], base["sV"] + (i - 2) * 3 + 3)
                    sy.wait_ge(sems["sA"], base["sA"] + (i - 2) * 2 + 1)
                sy.dma_start(out=xbuf[:, sl:sl + DM],
                             in_=x_in[i * 128:(i + 1) * 128, :]).then_inc(sems["aL"], 16)
        aL0 = P.cnt["aL"]
        P.bump("aL", NT * 16)

        @block.vector
        def _(v):
            for i in range(NT):
                sl = (i % 2) * DM
                sq = 4 + (i % 2) * 3
                v.wait_ge(sems["aL"], aL0 + (i + 1) * 16)
                xv = xbuf[:, sl:sl + DM]
                v.reduce_sum(st[:, 0:1], xv, axis=AX.X)
                v.drain()
                v.tensor_scalar_mul(st[:, 1:2], st[:, 0:1], 1.0 / DM)
                v.drain()
                P.inc(v.tensor_mul(st[:, 2:3], st[:, 0:1], st[:, 1:2]), "sV")
                v.drain()
                v.wait_ge(sems["sA"], base["sA"] + i * 2 + 1)
                P.inc(v.tensor_sub(st[:, 3:4], st[:, sq:sq + 1], st[:, 2:3]), "sV")
                v.wait_ge(sems["sA"], base["sA"] + i * 2 + 2)
                if i >= 2:
                    v.wait_ge(sems["sS"], base["sS"] + (i - 1) * 16)
                P.inc(v.tensor_scalar(xnbuf[:, sl:sl + DM], xv, st[:, 1:2],
                                      st[:, 6:7], ALU.subtract, ALU.mult), "sV")

        @block.scalar
        def _(s):
            for i in range(NT):
                sl = (i % 2) * DM
                sq = 4 + (i % 2) * 3
                s.wait_ge(sems["aL"], aL0 + (i + 1) * 16)
                P.inc(s.activation(out=sqbuf[:], in_=xbuf[:, sl:sl + DM],
                                   func=AF.Square, accum_out=st[:, sq:sq + 1]), "sA")
                s.wait_ge(sems["sV"], base["sV"] + i * 3 + 2)
                s.activation(out=st[:, 5:6], in_=st[:, 3:4], func=AF.Ln,
                             scale=1.0 / DM, bias=eps_col[:])
                P.inc(s.activation(out=st[:, 6:7], in_=st[:, 5:6],
                                   func=AF.Exp, scale=-0.5), "sA")
                if i >= 1:
                    s.wait_ge(sems["sV"], base["sV"] + (i - 1) * 3 + 3)
                    s.dma_start(out=XN_TM[(i - 1) * 128:i * 128, :],
                                in_=xnbuf[:, ((i - 1) % 2) * DM:((i - 1) % 2) * DM + DM]
                                ).then_inc(sems["sS"], 16)
            s.wait_ge(sems["sV"], base["sV"] + (NT - 1) * 3 + 3)
            s.dma_start(out=XN_TM[(NT - 1) * 128:NT * 128, :],
                        in_=xnbuf[:, ((NT - 1) % 2) * DM:((NT - 1) % 2) * DM + DM]
                        ).then_inc(sems["sS"], 16)
        P.bump("sS", NT * 16)


def phase_precast(nc, P, block, WoutT, W2T2, WOB, W2B, m_normw, bufs):
    sems = P.sem
    NTOT = NWO + NKC_W2
    f32buf, bfbuf = bufs
    if True:
        base = P.snap()

        @block.sync
        def _(sy):
            for i in range(NTOT):
                sl = (i % 2) * DM
                if i >= 2:
                    sy.wait_ge(sems["sG"], base["sG"] + (i - 2) + 1)
                src = (WoutT[i * 128:(i + 1) * 128, :] if i < NWO
                       else W2T2[(i - NWO) * 128:(i - NWO + 1) * 128, :])
                sy.dma_start(out=f32buf[:, sl:sl + DM], in_=src).then_inc(sems["pc"], 16)
        pc0 = P.cnt["pc"]
        P.bump("pc", NTOT * 16)

        @block.gpsimd
        def _(g):
            for i in range(NTOT):
                sl = (i % 2) * DM
                g.wait_ge(sems["pc"], pc0 + (i + 1) * 16)
                if i >= 2:
                    g.wait_ge(sems["pS"], base["pS"] + (i - 1) * 16)
                if i < NWO:
                    ins = g.tensor_scalar(bfbuf[:, sl:sl + DM], f32buf[:, sl:sl + DM],
                                          m_normw[:, i:i + 1], None, ALU.mult)
                else:
                    ins = g.tensor_copy(bfbuf[:, sl:sl + DM], f32buf[:, sl:sl + DM])
                P.inc(ins, "sG")

        @block.scalar
        def _(s):
            for i in range(NTOT):
                sl = (i % 2) * DM
                s.wait_ge(sems["sG"], base["sG"] + i + 1)
                dst = WOB[i] if i < NWO else W2B[i - NWO]
                s.dma_start(out=dst, in_=bfbuf[:, sl:sl + DM]).then_inc(sems["pS"], 16)
        P.bump("pS", NTOT * 16)


def phase_inproj(nc, P, block, WinT4, XN_TM, ZX_FM, bufs, ps):
    sems = P.sem
    xnf, wbuf, evb = bufs
    base = P.snap()

    @block.sync
    def _(sy):
        for kc in range(NKC):
            sy.dma_start(out=xnf[:, kc * L:(kc + 1) * L],
                         in_=XN_TM[:, kc * 128:(kc + 1) * 128],
                         transpose=True).then_inc(sems["tL"], 16)
    P.bump("tL", NKC * 16)

    @block.gpsimd
    def _(g):
        for ot in range(NOT_IN):
            sl = (ot % 2) * DM
            if ot >= 2:
                g.wait_ge(sems["sT"], base["sT"] + (ot - 2) + 1)
            g.dma_start(out=wbuf[:, sl:sl + DM],
                        in_=WinT4[ot]).then_inc(sems["wL"], 16)
    wL0 = P.cnt["wL"]
    P.bump("wL", NOT_IN * 16)

    @block.tensor
    def _(t):
        t.wait_ge(sems["tL"], base["tL"] + NKC * 16)
        for ot in range(NOT_IN):
            sl = (ot % 2) * DM
            pslot = (ot % 2) * L
            t.wait_ge(sems["wL"], wL0 + (ot + 1) * 16)
            if ot >= 2:
                t.wait_ge(sems["sV"], base["sV"] + (ot - 2) + 1)
            last = None
            for kc in range(NKC):
                lhsT = wbuf[:, sl + kc * 128: sl + (kc + 1) * 128]
                for tt in range(4):
                    last = t.matmul(ps[:, pslot + tt * 512: pslot + (tt + 1) * 512],
                                    lhsT, xnf[:, kc * L + tt * 512: kc * L + (tt + 1) * 512],
                                    start=(kc == 0), stop=(kc == NKC - 1))
            P.inc(last, "sT")

    @block.vector
    def _(v):
        for ot in range(NOT_IN):
            sl = (ot % 2) * L
            v.wait_ge(sems["sT"], base["sT"] + ot + 1)
            if ot >= 2:
                v.wait_ge(sems["sS"], base["sS"] + (ot - 1) * 16)
            P.inc(v.tensor_copy(evb[:, sl:sl + L], ps[:, sl:sl + L]), "sV")

    @block.scalar
    def _(s):
        for ot in range(NOT_IN):
            sl = (ot % 2) * L
            s.wait_ge(sems["sV"], base["sV"] + ot + 1)
            s.dma_start(out=ZX_FM[ot * 128:(ot + 1) * 128, :],
                        in_=evb[:, sl:sl + L]).then_inc(sems["sS"], 16)
    P.bump("sS", NOT_IN * 16)


def phase_conv(nc, P, block, ZX_FM, XC_FM, XD_FM, BC_FM, m_convw, m_convb,
               m_Drep, ident_bf, bufs, ps):
    sems = P.sem
    xt, dg, cvb, xdb = bufs
    base = P.snap()

    @block.vector
    def _(v):
        v.memset(xt[:, 0:3], 0.0)
        P.inc(v.memset(xt[:, 2054:2057], 0.0), "sV")

    @block.gpsimd
    def _(g):
        for ct in range(NCT):
            sl = (ct % 2) * 2054
            if ct >= 2:
                g.wait_ge(sems["sT"], base["sT"] + (ct - 2) + 1)
            g.dma_start(out=xt[:, sl + 3: sl + 3 + L],
                        in_=ZX_FM[DI + ct * 128: DI + (ct + 1) * 128, :]
                        ).then_inc(sems["aL"], 16)
    aL0 = P.cnt["aL"]
    P.bump("aL", NCT * 16)

    @block.vector
    def _(v):
        for ct in range(NCT + 2):
            if ct < NCT:
                dsl = (ct % 2) * 512
                if ct >= 2:
                    v.wait_ge(sems["sT"], base["sT"] + (ct - 2) + 1)
                for k in range(CONVK):
                    v.tensor_scalar(dg[:, dsl + k * 128: dsl + (k + 1) * 128],
                                    ident_bf[:],
                                    m_convw[:, ct * CONVK + k: ct * CONVK + k + 1],
                                    None, ALU.mult)
                P.inc(v.drain(), "sG")
            ct2 = ct - 2
            if 0 <= ct2 < NXT:
                sl = (ct2 % 2) * L
                v.wait_ge(sems["sA"], base["sA"] + ct2 + 1)
                if ct2 >= 2:
                    v.wait_ge(sems["sS"], base["sS"] + _conv_nstores(ct2 - 1) * 16)
                v.tensor_scalar(xdb[:, sl:sl + L], cvb[:, sl:sl + L],
                                m_Drep[:, ct2:ct2 + 1], None, ALU.mult)
                P.inc(v.drain(), "sV")

    @block.tensor
    def _(t):
        t.wait_ge(sems["sV"], base["sV"] + 1)
        for ct in range(NCT):
            sl = (ct % 2) * 2054
            dsl = (ct % 2) * 512
            psl = (ct % 2) * L
            t.wait_ge(sems["aL"], aL0 + (ct + 1) * 16)
            t.wait_ge(sems["sG"], base["sG"] + ct + 1)
            if ct >= 2:
                t.wait_ge(sems["sA"], base["sA"] + (ct - 2) + 1)
            last = None
            for k in range(CONVK):
                lhsT = dg[:, dsl + k * 128: dsl + (k + 1) * 128]
                for tt in range(4):
                    last = t.matmul(ps[:, psl + tt * 512: psl + (tt + 1) * 512],
                                    lhsT,
                                    xt[:, sl + k + tt * 512: sl + k + tt * 512 + 512],
                                    start=(k == 0), stop=(k == CONVK - 1))
            P.inc(last, "sT")

    @block.scalar
    def _(s):
        for ct in range(NCT):
            sl = (ct % 2) * L
            s.wait_ge(sems["sT"], base["sT"] + ct + 1)
            if ct >= 2:
                s.wait_ge(sems["sS"], base["sS"] + _conv_nstores(ct - 1) * 16)
            s.activation(out=cvb[:, sl:sl + L], in_=ps[:, sl:sl + L],
                         func=AF.Silu, bias=m_convb[:, ct:ct + 1], scale=1.0)
            P.inc(s.drain(), "sA")

    @block.sync
    def _(sy):
        for ct in range(NCT):
            sl = (ct % 2) * L
            sy.wait_ge(sems["sA"], base["sA"] + ct + 1)
            if ct < NXT:
                sy.wait_ge(sems["sV"], base["sV"] + 1 + ct + 1)
                sy.dma_start(out=XC_FM[ct * 128:(ct + 1) * 128, :],
                             in_=cvb[:, sl:sl + L]).then_inc(sems["sS"], 16)
                sy.dma_start(out=XD_FM[ct * 128:(ct + 1) * 128, :],
                             in_=xdb[:, sl:sl + L]).then_inc(sems["sS"], 16)
            else:
                sy.dma_start(out=BC_FM[(ct - NXT) * 128:(ct - NXT + 1) * 128, :],
                             in_=cvb[:, sl:sl + L]).then_inc(sems["sS"], 16)
    P.bump("sS", _conv_nstores(NCT) * 16)


def _conv_nstores(ct):
    return 2 * min(ct, NXT) + max(ct - NXT, 0)


def phase_dt(nc, P, block, ZX_FM, CS_HI, CS_LO, consts, res, scratch, psb, ps2):
    sems = P.sem
    (vec_dtb, vec_nA, one_col, ident_f32, ones_f32, zeros128) = consts
    (dt_tm, expcs_tm, dexp_tm, expTrep, ncs_hi, ncs_lo) = res
    (dtraw, sp1, sp2, cs_fm, dexp_fm, cshi, cslo, trow) = scratch
    base = P.snap()

    @block.gpsimd
    def _(g):
        g.dma_start(out=dtraw[:], in_=ZX_FM[DIN - 128:DIN, :]).then_inc(sems["aL"], 16)
    aL0 = P.cnt["aL"]
    P.bump("aL", 16)

    @block.scalar
    def _(s):
        s.wait_ge(sems["aL"], aL0 + 16)
        s.activation(out=sp1[:], in_=dtraw[:], func=AF.Relu, bias=vec_dtb[:])
        s.activation(out=sp2[:], in_=dtraw[:], func=AF.Abs, bias=vec_dtb[:])
        s.activation(out=sp2[:], in_=sp2[:], func=AF.Exp, scale=-1.0)
        P.inc(s.activation(out=sp2[:], in_=sp2[:], func=AF.Ln, bias=one_col[:]), "sA")

    @block.vector
    def _(v):
        v.wait_ge(sems["sA"], base["sA"] + 1)
        v.tensor_add(sp1[:], sp1[:], sp2[:])
        v.drain()
        v.tensor_scalar(sp2[:], sp1[:], vec_nA[:], None, ALU.mult)
        v.drain()
        for c in range(NCH):
            v.tensor_tensor_scan(out=cs_fm[:, c * 128:(c + 1) * 128],
                                 data0=sp2[:, c * 128:(c + 1) * 128],
                                 data1=zeros128[:], initial=0.0,
                                 op0=ALU.add, op1=ALU.add)
        v.drain()
        v.tensor_copy(cshi[:], cs_fm[:])
        v.drain()
        v.tensor_sub(cslo[:], cs_fm[:], cshi[:])
        v.drain()
        v.tensor_scalar_mul(ncs_hi[:], cshi[:], -1.0)
        P.inc(v.tensor_scalar_mul(ncs_lo[:], cslo[:], -1.0), "sV")

    @block.sync
    def _(sy):
        sy.wait_ge(sems["sV"], base["sV"] + 1)
        sy.dma_start(out=CS_HI, in_=cshi[:]).then_inc(sems["sS"], 16)
        sy.dma_start(out=CS_LO, in_=cslo[:]).then_inc(sems["sS"], 16)
    P.bump("sS", 2 * 16)

    @block.scalar
    def _(s):
        s.wait_ge(sems["sV"], base["sV"] + 1)
        for c in range(NCH):
            ins = s.activation(out=dexp_fm[:, c * 128:(c + 1) * 128],
                               in_=cs_fm[:, c * 128:(c + 1) * 128], func=AF.Exp,
                               scale=-1.0,
                               bias=cs_fm[:, c * 128 + 127:c * 128 + 128])
            P.inc(ins, "sA")

    vb = P.cnt["sV"]
    ab = P.cnt["sA"]
    tb = P.cnt["sT"]

    @block.tensor
    def _(t):
        for c in range(NCH):
            t.wait_ge(sems["sA"], ab - NCH + c + 1)
            if c >= 1:
                t.wait_ge(sems["sV"], vb + c)
                t.wait_ge(sems["sA"], ab + c)
            cs_c = cs_fm[:, c * 128:(c + 1) * 128]
            t.transpose(psb[:, 0:128], sp1[:, c * 128:(c + 1) * 128], ident_f32[:])
            t.transpose(psb[:, 512:640], cs_c, ident_f32[:])
            t.transpose(psb[:, 1024:1152], dexp_fm[:, c * 128:(c + 1) * 128],
                        ident_f32[:])
            P.inc(t.transpose(psb[0:1, 1536:1664],
                              cs_fm[:, c * 128 + 127:c * 128 + 128],
                              ident_f32[:]), "sT")
            t.wait_ge(sems["sV"], vb + c + 1)
            P.inc(t.matmul(ps2[:, (c % 2) * 512:(c % 2) * 512 + 128],
                           ones_f32[0:1, 0:128], trow[0:1, :],
                           start=True, stop=True), "sT")

    @block.vector
    def _(v):
        for c in range(NCH):
            v.wait_ge(sems["sT"], tb + 2 * c + 1)
            v.tensor_copy(dt_tm[:, c * 128:(c + 1) * 128], psb[:, 0:128])
            v.tensor_copy(dexp_tm[:, c * 128:(c + 1) * 128], psb[:, 1024:1152])
            P.inc(v.tensor_copy(trow[0:1, :], psb[0:1, 1536:1664]), "sV")

    @block.scalar
    def _(s):
        for c in range(NCH):
            s.wait_ge(sems["sT"], tb + 2 * c + 1)
            s.activation(out=expcs_tm[:, c * 128:(c + 1) * 128],
                         in_=psb[:, 512:640], func=AF.Exp)
            s.wait_ge(sems["sT"], tb + 2 * c + 2)
            P.inc(s.activation(out=expTrep[:, c * 128:(c + 1) * 128],
                               in_=ps2[:, (c % 2) * 512:(c % 2) * 512 + 128],
                               func=AF.Exp), "sA")
    P.bump("sA", 0)


def phase_ssd(nc, P, block, XC_FM, XD_FM, ZX_FM, BC_FM, CS_HI, CS_LO, YG_TM,
              consts, res, bufs, psA, psT):
    sems = P.sem
    (ident_bf, ones_bf, eps_col, m0mask) = consts
    (dt_bf, expcs_tm, dexp_bf, expTrep, ncs_hi, ncs_lo, R, R16) = res
    (sx, sxd, sz, sxdt, sxtl, sm, sd, ss_sb, sbtm, sbcf, stmp, sfh, sfl,
     srms) = bufs

    @block.vector
    def _(v):
        v.memset(R[:], 0.0)
        P.inc(v.memset(R16[:], 0.0), "sV")

    base = P.snap()
    tb, vb, ab, gb, lb, fb, sb = (base["sT"], base["sV"], base["sA"],
                                  base["sG"], base["tL"], base["aL"],
                                  base["sS"])
    TPC, VPC, APC, GPC, LPC, FPC, SPC = 34, 60, 11, 10, 48, 288, 16

    def tOf(c, off=0):
        return tb + TPC * c + off

    def vOf(c, off=0):
        return vb + VPC * c + off

    def aAOf(c, off=0):
        return ab + APC * c + off

    def gOf(c, off=0):
        return gb + GPC * c + off

    def lOf(c, off=0):
        return lb + LPC * c + off

    def aOf(c, off=0):
        return fb + FPC * c + off

    psS = psA[:, 0:1024]
    psY1 = psA[:, 0:1024]
    psY2 = psA[:, 1024:2048]
    psDel = psA[:, 2048:3072]

    def emit_sync(c):
        @block.sync
        def _(sy):
            if c >= 1:
                sy.wait_ge(sems["sV"], vOf(c - 1, 60))
                sy.dma_start(out=YG_TM[(c - 1) * 128:c * 128, :],
                             in_=sx[:]).then_inc(sems["sS"], 16)
            sy.wait_ge(sems["sS"], sb + 16 * c)
            sy.dma_start(out=sx[:], in_=XC_FM[:, c * 128:(c + 1) * 128],
                         transpose=True).then_inc(sems["tL"], 16)
            if c >= 1:
                sy.wait_ge(sems["sV"], vOf(c - 1, 55))
            sy.dma_start(out=sxd[:], in_=XD_FM[:, c * 128:(c + 1) * 128],
                         transpose=True).then_inc(sems["tL"], 16)
            if c >= 1:
                sy.wait_ge(sems["sV"], vOf(c - 1, 59))
            sy.dma_start(out=sz[:], in_=ZX_FM[0:DI, c * 128:(c + 1) * 128],
                         transpose=True).then_inc(sems["tL"], 16)
            sy.wait_ge(sems["sT"], tOf(c, 0))
            bcs = BC_FM[:, c * 128:(c + 1) * 128]
            sy.dma_start(out=sbcf[:].rearrange("p (g t) -> p g t", g=16)[:, 0:8, :],
                         in_=bcs[0:1024].rearrange("(g p) t -> p g t", p=128)
                         ).then_inc(sems["aL"], 16)
            sy.dma_start(out=sbcf[:].rearrange("p (g t) -> p g t", g=16)[:, 8:16, :],
                         in_=bcs[1024:2048].rearrange("(g p) t -> p g t", p=128)
                         ).then_inc(sems["aL"], 16)
            for gg in range(8):
                if gg >= 2:
                    sy.wait_ge(sems["sT"], tOf(c, 3 + 4 * (gg - 2)))
                fsl = (gg % 2) * 2048
                sy.dma_start(out=sfh[:, fsl:fsl + 2048]
                             .rearrange("o (h t) -> o h t", h=16),
                             in_=CS_HI[16 * gg:16 * gg + 16, c * 128:(c + 1) * 128]
                             .unsqueeze(0)).then_inc(sems["aL"], 16)
                sy.dma_start(out=sfl[:, fsl:fsl + 2048]
                             .rearrange("o (h t) -> o h t", h=16),
                             in_=CS_LO[16 * gg:16 * gg + 16, c * 128:(c + 1) * 128]
                             .unsqueeze(0)).then_inc(sems["aL"], 16)
        P.bump("tL", LPC)
        P.bump("aL", FPC)
        if c >= 1:
            P.bump("sS", SPC)

    def emit_final_store():
        @block.sync
        def _(sy):
            sy.wait_ge(sems["sV"], vOf(NCH - 1, 60))
            sy.dma_start(out=YG_TM[(NCH - 1) * 128:NCH * 128, :],
                         in_=sx[:]).then_inc(sems["sS"], 16)
        P.bump("sS", SPC)

    def emit_tensor(c):
        @block.tensor
        def _(t):
            if c >= 1:
                t.wait_ge(sems["sV"], vOf(c - 1, 58))
                t.wait_ge(sems["sG"], gOf(c - 1, 10))
            else:
                t.wait_ge(sems["sV"], vb)
            t.wait_ge(sems["aL"], aOf(c, 32))
            last = None
            for g in range(8):
                last = t.transpose(psT[:, g * 128:(g + 1) * 128],
                                   sbcf[:, g * 128:(g + 1) * 128], ident_bf[:])
            P.inc(last, "sT")
            last = None
            for g in range(8):
                last = t.matmul(psS[:, g * 128:(g + 1) * 128],
                                sbcf[:, g * 128:(g + 1) * 128],
                                sbcf[:, (8 + g) * 128:(9 + g) * 128],
                                start=True, stop=True)
            P.inc(last, "sT")
            for gg in range(8):
                t.wait_ge(sems["aL"], aOf(c, 32 + 32 * (gg + 1)))
                if gg >= 1:
                    t.wait_ge(sems["sV"], vOf(c, 9 + 7 * (gg - 1)))
                fsl = (gg % 2) * 2048
                last = None
                for q in range(4):
                    dsl = 1536 + q * 512
                    identq = (ident_bf[:, 16 * gg + 4 * q:16 * gg + 4 * q + 4]
                              .unsqueeze(2).broadcast_to([128, 4, 128]))
                    ncol = c * 128
                    t.matmul(psA[:, dsl:dsl + 512],
                             ncs_hi[:, ncol:ncol + 128], identq,
                             start=True, stop=False)
                    t.matmul(psA[:, dsl:dsl + 512],
                             ncs_lo[:, ncol:ncol + 128], identq,
                             start=False, stop=False)
                    t.matmul(psA[:, dsl:dsl + 512],
                             ones_bf[0:1, 0:128],
                             sfh[0:1, fsl + q * 512:fsl + (q + 1) * 512],
                             start=False, stop=False)
                    last = t.matmul(psA[:, dsl:dsl + 512],
                                    ones_bf[0:1, 0:128],
                                    sfl[0:1, fsl + q * 512:fsl + (q + 1) * 512],
                                    start=False, stop=True)
                P.inc(last, "sT")
                t.wait_ge(sems["sV"], vOf(c, 3 + 7 * gg))
                cfm = sbcf[:, (8 + gg) * 128:(9 + gg) * 128]
                last = None
                for hh in range(16):
                    h = 16 * gg + hh
                    last = t.matmul(psY2[:, hh * 64:(hh + 1) * 64], cfm,
                                    R16[:, h * 64:(h + 1) * 64],
                                    start=True, stop=True)
                P.inc(last, "sT")
                t.wait_ge(sems["sV"], vOf(c, 4 + 7 * gg))
                if gg >= 1:
                    t.wait_ge(sems["sV"], vOf(c, 7 + 7 * (gg - 1)))
                last = None
                for hh in range(16):
                    h = 16 * gg + hh
                    last = t.matmul(psY1[:, hh * 64:(hh + 1) * 64],
                                    sm[:, hh * 128:(hh + 1) * 128],
                                    sxdt[:, h * 64:(h + 1) * 64],
                                    start=True, stop=True)
                P.inc(last, "sT")
                if gg == 0:
                    t.wait_ge(sems["sG"], gOf(c, 2))
                    t.wait_ge(sems["sV"], vOf(c, 1))
                last = None
                for hh in range(16):
                    h = 16 * gg + hh
                    last = t.matmul(psDel[:, hh * 64:(hh + 1) * 64],
                                    sbtm[:, gg * 128:(gg + 1) * 128],
                                    sxtl[:, h * 64:(h + 1) * 64],
                                    start=True, stop=True)
                P.inc(last, "sT")

    def emit_vector(c):
        @block.vector
        def _(v):
            v.wait_ge(sems["sT"], tOf(c, 1))
            for g in range(8):
                v.tensor_copy(sbtm[:, g * 128:(g + 1) * 128],
                              psT[:, g * 128:(g + 1) * 128])
            P.inc(v.drain(), "sV")
            v.wait_ge(sems["sT"], tOf(c, 2))
            v.tensor_copy(ss_sb[:], psS[:])
            P.inc(v.drain(), "sV")
            for gg in range(8):
                v.wait_ge(sems["sT"], tOf(c, 3 + 4 * gg))
                if gg >= 1:
                    v.wait_ge(sems["sA"], aAOf(c, gg))
                v.tensor_add(sd[:].rearrange("p (a b) -> p a b", a=16),
                             psA[:, 1536:3584].rearrange("p (a b) -> p a b", a=16),
                             m0mask[:].unsqueeze(1).broadcast_to([128, 16, 128]))
                P.inc(v.drain(), "sV")
                v.wait_ge(sems["sA"], aAOf(c, 1 + gg))
                if gg >= 1:
                    v.wait_ge(sems["sT"], tOf(c, 5 + 4 * (gg - 1)))
                v.tensor_mul(sm[:].rearrange("p (a b) -> p a b", a=16),
                             sd[:].rearrange("p (a b) -> p a b", a=16),
                             ss_sb[:, gg * 128:(gg + 1) * 128].unsqueeze(1)
                             .broadcast_to([128, 16, 128]))
                P.inc(v.drain(), "sV")
                v.wait_ge(sems["sT"], tOf(c, 4 + 4 * gg))
                ecol = c * 128 + 16 * gg
                P.inc(v.tensor_mul(stmp[:].rearrange("p (a b) -> p a b", a=16),
                                   psY2[:].rearrange("p (a b) -> p a b", a=16),
                                   expcs_tm[:, ecol:ecol + 16].unsqueeze(2)
                                   .broadcast_to([128, 16, 64])), "sV")
                v.drain()
                P.inc(v.tensor_add(stmp[:], stmp[:],
                                   sxd[:, gg * 1024:(gg + 1) * 1024]), "sV")
                v.drain()
                v.wait_ge(sems["sT"], tOf(c, 5 + 4 * gg))
                P.inc(v.tensor_add(sxdt[:, gg * 1024:(gg + 1) * 1024],
                                   stmp[:], psY1[:]), "sV")
                v.drain()
                P.inc(v.tensor_mul(stmp[:].rearrange("p (a b) -> p a b", a=16),
                                   R[:, gg * 1024:(gg + 1) * 1024]
                                   .rearrange("p (a b) -> p a b", a=16),
                                   expTrep[:, ecol:ecol + 16].unsqueeze(2)
                                   .broadcast_to([128, 16, 64])), "sV")
                v.drain()
                v.wait_ge(sems["sT"], tOf(c, 6 + 4 * gg))
                v.tensor_add(R[:, gg * 1024:(gg + 1) * 1024],
                             stmp[:], psDel[:])
                P.inc(v.drain(), "sV")
            v.wait_ge(sems["sA"], aAOf(c, 9))
            v.wait_ge(sems["sT"], tOf(c, 34))
            v.tensor_mul(sxtl[:], sxdt[:], sz[:])
            P.inc(v.drain(), "sV")
            v.wait_ge(sems["sA"], aAOf(c, 11))
            v.tensor_scalar(sx[:], sxtl[:], srms[:, 1:2], None, ALU.mult)
            P.inc(v.drain(), "sV")

    def emit_pool(c):
        @block.gpsimd
        def _(g):
            g.wait_ge(sems["tL"], lOf(c, 16))
            if c >= 1:
                g.wait_ge(sems["sV"], vOf(c - 1, 59))
            dcol = c * 128
            g.tensor_mul(sxdt[:].rearrange("p (a b) -> p a b", a=128),
                         sx[:].rearrange("p (a b) -> p a b", a=128),
                         dt_bf[:, dcol:dcol + 128].unsqueeze(2)
                         .broadcast_to([128, 128, 64]))
            P.inc(g.drain(), "sG")
            if c >= 1:
                g.wait_ge(sems["sV"], vOf(c - 1, 60))
            g.tensor_mul(sxtl[:].rearrange("p (a b) -> p a b", a=128),
                         sxdt[:].rearrange("p (a b) -> p a b", a=128),
                         dexp_bf[:, dcol:dcol + 128].unsqueeze(2)
                         .broadcast_to([128, 128, 64]))
            P.inc(g.drain(), "sG")
            for gg in range(8):
                g.wait_ge(sems["sV"], vOf(c, 9 + 7 * gg))
                g.tensor_copy(R16[:, gg * 1024:(gg + 1) * 1024],
                              R[:, gg * 1024:(gg + 1) * 1024])
                P.inc(g.drain(), "sG")

    def emit_scalar(c):
        @block.scalar
        def _(s):
            for gg in range(8):
                s.wait_ge(sems["sV"], vOf(c, 3 + 7 * gg))
                s.activation(out=sd[:], in_=sd[:], func=AF.Exp)
                P.inc(s.drain(), "sA")
            s.wait_ge(sems["tL"], lOf(c, 48))
            s.activation(out=sz[:], in_=sz[:], func=AF.Silu)
            P.inc(s.drain(), "sA")
            s.wait_ge(sems["sV"], vOf(c, 59))
            P.inc(s.activation(out=sx[:], in_=sxtl[:], func=AF.Square,
                               accum_out=srms[:, 0:1]), "sA")
            s.activation(out=srms[:, 1:2], in_=srms[:, 0:1], func=AF.Ln,
                         scale=1.0 / DI, bias=eps_col[:])
            s.activation(out=srms[:, 1:2], in_=srms[:, 1:2], func=AF.Exp,
                         scale=-0.5)
            P.inc(s.drain(), "sA")

    for c in range(NCH):
        emit_sync(c)
        emit_tensor(c)
        emit_vector(c)
        emit_pool(c)
        emit_scalar(c)
        if SSD_BARRIER:
            P.barrier()
    emit_final_store()


def phase_outproj(nc, P, block, YG_TM, WOB, PROJ_FM, bufs, ps):
    sems = P.sem
    ygf, wbuf, evb = bufs
    for half in range(2):
        base = P.snap()
        t0, t1 = half * 1024, (half + 1) * 1024

        @block.sync
        def _(sy):
            for kc in range(NWO):
                sy.dma_start(out=ygf[:, kc * 1024:(kc + 1) * 1024],
                             in_=YG_TM[t0:t1, kc * 128:(kc + 1) * 128],
                             transpose=True).then_inc(sems["tL"], 16)
        P.bump("tL", NWO * 16)

        @block.scalar
        def _(s):
            for ot in range(32):
                sl = (ot % 2) * DI
                if ot >= 2:
                    s.wait_ge(sems["sT"], base["sT"] + (ot - 2) + 1)
                for qq in range(4):
                    s.dma_start(out=wbuf[:, sl + qq * 2048: sl + (qq + 1) * 2048]
                                .rearrange("p (kc o) -> p kc o", kc=16),
                                in_=WOB[qq * 16:(qq + 1) * 16, :,
                                        ot * 128:(ot + 1) * 128]
                                .rearrange("kc p o -> p kc o")
                                ).then_inc(sems["wL"], 16)
        wL0 = P.cnt["wL"]
        P.bump("wL", 32 * 4 * 16)

        @block.tensor
        def _(t):
            t.wait_ge(sems["tL"], base["tL"] + NWO * 16)
            for ot in range(32):
                sl = (ot % 2) * DI
                psl = (ot % 2) * 1024
                t.wait_ge(sems["wL"], wL0 + (ot + 1) * 64)
                if ot >= 2:
                    t.wait_ge(sems["sV"], base["sV"] + (ot - 2) + 1)
                last = None
                for kc in range(NWO):
                    lhsT = wbuf[:, sl + kc * 128: sl + (kc + 1) * 128]
                    for tt in range(2):
                        last = t.matmul(ps[:, psl + tt * 512: psl + (tt + 1) * 512],
                                        lhsT,
                                        ygf[:, kc * 1024 + tt * 512: kc * 1024 + (tt + 1) * 512],
                                        start=(kc == 0), stop=(kc == NWO - 1))
                P.inc(last, "sT")

        @block.vector
        def _(v):
            for ot in range(32):
                psl = (ot % 2) * 1024
                v.wait_ge(sems["sT"], base["sT"] + ot + 1)
                if ot >= 2:
                    v.wait_ge(sems["sS"], base["sS"] + (ot - 1) * 16)
                P.inc(v.tensor_copy(evb[:, psl:psl + 1024], ps[:, psl:psl + 1024]),
                      "sV")

        @block.gpsimd
        def _(g):
            for ot in range(32):
                psl = (ot % 2) * 1024
                g.wait_ge(sems["sV"], base["sV"] + ot + 1)
                g.dma_start(out=PROJ_FM[ot * 128:(ot + 1) * 128, t0:t1],
                            in_=evb[:, psl:psl + 1024]).then_inc(sems["sS"], 16)
        P.bump("sS", 32 * 16)

        P.barrier()


def phase_ffn1(nc, P, block, XN_TM, W1T4, W3T4, a1T, a3T, b1T_in, b3T_in,
               H_FM, bufs, psG, psU, psla):
    sems = P.sem
    xnf, w1b, w3b, sgb, hvb, a1s, a3s, la1, la3, b1o, b3o = bufs

    base = P.snap()

    @block.sync
    def _(sy):
        for kc in range(NKC):
            sy.dma_start(out=xnf[:, kc * L:(kc + 1) * L],
                         in_=XN_TM[:, kc * 128:(kc + 1) * 128],
                         transpose=True).then_inc(sems["tL"], 16)
    P.bump("tL", NKC * 16)

    @block.gpsimd
    def _(g):
        g.dma_start(out=a1s[:], in_=a1T).then_inc(sems["aL"], 16)
        g.dma_start(out=a3s[:], in_=a3T).then_inc(sems["aL"], 16)
    P.bump("aL", 2 * 16)

    @block.tensor
    def _(t):
        if not FFN1_PREPASS:
            return
        t.wait_ge(sems["tL"], P.cnt["tL"])
        t.wait_ge(sems["aL"], P.cnt["aL"])
        for which, asb in ((0, a1s), (1, a3s)):
            if which == 1:
                t.wait_ge(sems["sV"], base["sV"] + 1)
            last = None
            for kc in range(NKC):
                for q in range(4):
                    last = t.matmul(psla[0:2, q * 512:(q + 1) * 512],
                                    asb[:, kc * 2:(kc + 1) * 2],
                                    xnf[:, kc * L + q * 512: kc * L + (q + 1) * 512],
                                    start=(kc == 0), stop=(kc == NKC - 1))
            P.inc(last, "sT")

    @block.vector
    def _(v):
        if not FFN1_PREPASS:
            return
        v.wait_ge(sems["sT"], base["sT"] + 1)
        P.inc(v.tensor_scalar(la1[:], psla[0:2, :], 0.5, None, ALU.mult), "sV")
        v.wait_ge(sems["sT"], base["sT"] + 2)
        P.inc(v.tensor_scalar(la3[:], psla[0:2, :], 0.5, None, ALU.mult), "sV")

    P.barrier()
    base = P.snap()

    @block.gpsimd
    def _(g):
        for ot in range(FFN1_NOT):
            sl = (ot % 2) * 1024
            if ot >= 2:
                g.wait_ge(sems["sT"], base["sT"] + 2 * (ot - 2) + 2)
            g.dma_start(out=w1b[:, sl:sl + 1024], in_=W1T4[ot]).then_inc(sems["wL"], 16)
            g.dma_start(out=w3b[:, sl:sl + 1024], in_=W3T4[ot]).then_inc(sems["wL"], 16)
            bsl = (ot % 2) * 128
            g.dma_start(out=b1o[0:2, bsl:bsl + 128],
                        in_=b1T_in[0:2, ot * 128:(ot + 1) * 128]).then_inc(sems["wL"], 16)
            g.dma_start(out=b3o[0:2, bsl:bsl + 128],
                        in_=b3T_in[0:2, ot * 128:(ot + 1) * 128]).then_inc(sems["wL"], 16)
    wL0 = P.cnt["wL"]
    P.bump("wL", FFN1_NOT * 64)

    @block.tensor
    def _(t):
        for ot in range(FFN1_NOT):
            sl = (ot % 2) * 1024
            t.wait_ge(sems["wL"], wL0 + (ot + 1) * 64)
            if ot >= 1:
                t.wait_ge(sems["sA"], base["sA"] + ot)
                t.wait_ge(sems["sV"], base["sV"] + ot)
            last = None
            for kc in range(NKC):
                lhsT = w1b[:, sl + (kc % 8) * 128: sl + (kc % 8 + 1) * 128]
                for q in range(4):
                    last = t.matmul(psG[:, q * 512:(q + 1) * 512], lhsT,
                                    xnf[:, kc * L + q * 512: kc * L + (q + 1) * 512],
                                    start=(kc == 0), stop=False)
            bsl = (ot % 2) * 128
            if FFN1_LORA:
                for q in range(4):
                    last = t.matmul(psG[:, q * 512:(q + 1) * 512],
                                    b1o[0:2, bsl:bsl + 128],
                                    la1[0:2, q * 512:(q + 1) * 512],
                                    start=False, stop=True)
            else:
                for q in range(4):
                    last = t.matmul(psG[:, q * 512:(q + 1) * 512],
                                    w1b[:, sl:sl + 128],
                                    xnf[:, q * 512:(q + 1) * 512],
                                    start=False, stop=True)
            P.inc(last, "sT")
            last = None
            for kc in range(NKC):
                lhsT = w3b[:, sl + (kc % 8) * 128: sl + (kc % 8 + 1) * 128]
                for q in range(4):
                    last = t.matmul(psU[:, q * 512:(q + 1) * 512], lhsT,
                                    xnf[:, kc * L + q * 512: kc * L + (q + 1) * 512],
                                    start=(kc == 0), stop=False)
            if FFN1_LORA:
                for q in range(4):
                    last = t.matmul(psU[:, q * 512:(q + 1) * 512],
                                    b3o[0:2, bsl:bsl + 128],
                                    la3[0:2, q * 512:(q + 1) * 512],
                                    start=False, stop=True)
            else:
                for q in range(4):
                    last = t.matmul(psU[:, q * 512:(q + 1) * 512],
                                    w3b[:, sl:sl + 128],
                                    xnf[:, q * 512:(q + 1) * 512],
                                    start=False, stop=True)
            P.inc(last, "sT")

    @block.scalar
    def _(s):
        for ot in range(FFN1_NOT):
            gsl = (ot % 2) * L
            s.wait_ge(sems["sT"], base["sT"] + 2 * ot + 1)
            if ot >= 2:
                s.wait_ge(sems["sV"], base["sV"] + ot - 1)
            P.inc(s.activation(out=sgb[:, gsl:gsl + L], in_=psG[:], func=AF.Silu),
                  "sA")

    @block.vector
    def _(v):
        for ot in range(FFN1_NOT):
            gsl = (ot % 2) * L
            v.wait_ge(sems["sT"], base["sT"] + 2 * ot + 2)
            v.wait_ge(sems["sA"], base["sA"] + ot + 1)
            if ot >= 2:
                v.wait_ge(sems["sS"], base["sS"] + (ot - 1) * 16)
            P.inc(v.tensor_mul(hvb[:, gsl:gsl + L], sgb[:, gsl:gsl + L], psU[:]),
                  "sV")

    @block.sync
    def _(sy):
        for ot in range(FFN1_NOT):
            gsl = (ot % 2) * L
            sy.wait_ge(sems["sV"], base["sV"] + ot + 1)
            sy.dma_start(out=H_FM[ot * 128:(ot + 1) * 128, :],
                         in_=hvb[:, gsl:gsl + L]).then_inc(sems["sS"], 16)
    P.bump("sS", FFN1_NOT * 16)


def phase_ffn2(nc, P, block, H_FM, W2B, a2T, b2T_in, G_FM, G_LO, bufs, ps, psla):
    sems = P.sem
    hb, w2b, evb, evlo, a2s, la2, b2s = bufs

    @block.gpsimd
    def _(g):
        g.dma_start(out=a2s[:], in_=a2T).then_inc(sems["aL"], 16)
        g.dma_start(out=b2s[:], in_=b2T_in).then_inc(sems["aL"], 16)
    P.bump("aL", 2 * 16)

    for sl4 in range(4):
        base = P.snap()
        t0 = sl4 * 512

        @block.sync
        def _(sy):
            for kc in range(NKC_H):
                sy.dma_start(out=hb[:, kc * 512:(kc + 1) * 512],
                             in_=H_FM[kc * 128:(kc + 1) * 128, t0:t0 + 512]
                             ).then_inc(sems["tL"], 16)
        P.bump("tL", NKC_H * 16)

        @block.scalar
        def _(s):
            for ot in range(32):
                wsl = (ot % 2) * (NKC_W2 * 128)
                if ot >= 2:
                    s.wait_ge(sems["sT"], base["sT"] + 1 + (ot - 2) + 1)
                for qq in range(4):
                    n0 = qq * 11
                    n1 = min(NKC_W2, (qq + 1) * 11)
                    s.dma_start(out=w2b[:, wsl + n0 * 128: wsl + n1 * 128]
                                .rearrange("p (kc o) -> p kc o", o=128),
                                in_=W2B[n0:n1, :, ot * 128:(ot + 1) * 128]
                                .rearrange("kc p o -> p kc o")
                                ).then_inc(sems["wL"], 16)
        wL0 = P.cnt["wL"]
        P.bump("wL", 32 * 4 * 16)

        @block.tensor
        def _(t):
            t.wait_ge(sems["tL"], base["tL"] + NKC_H * 16)
            t.wait_ge(sems["aL"], P.cnt["aL"])
            last = None
            for kc in range(NKC_H):
                last = t.matmul(psla[0:2, 0:512], a2s[:, kc * 2:(kc + 1) * 2],
                                hb[:, kc * 512:(kc + 1) * 512],
                                start=(kc == 0), stop=(kc == NKC_H - 1))
            P.inc(last, "sT")

        @block.vector
        def _(v):
            v.wait_ge(sems["sT"], base["sT"] + 1)
            P.inc(v.tensor_scalar(la2[:], psla[0:2, 0:512], 0.5, None, ALU.mult),
                  "sV")

        @block.tensor
        def _(t):
            t.wait_ge(sems["sV"], base["sV"] + 1)
            for ot in range(32):
                wsl = (ot % 2) * (NKC_W2 * 128)
                psl = (ot % 2) * 512
                t.wait_ge(sems["wL"], wL0 + (ot + 1) * 64)
                if ot >= 2:
                    t.wait_ge(sems["sV"], base["sV"] + 1 + (ot - 2) + 1)
                last = None
                for kc in range(NKC_H):
                    lhsT = w2b[:, wsl + (kc % NKC_W2) * 128:
                               wsl + (kc % NKC_W2 + 1) * 128]
                    last = t.matmul(ps[:, psl:psl + 512], lhsT,
                                    hb[:, kc * 512:(kc + 1) * 512],
                                    start=(kc == 0), stop=False)
                last = t.matmul(ps[:, psl:psl + 512],
                                b2s[0:2, ot * 128:(ot + 1) * 128], la2[:],
                                start=False, stop=True)
                P.inc(last, "sT")

        @block.vector
        def _(v):
            for ot in range(32):
                psl = (ot % 2) * 512
                esl = (ot % 2) * 512
                v.wait_ge(sems["sT"], base["sT"] + 1 + ot + 1)
                if ot >= 2:
                    v.wait_ge(sems["sS"], base["sS"] + (ot - 1) * 32)
                v.tensor_copy(evb[:, esl:esl + 512], ps[:, psl:psl + 512])
                v.drain()
                P.inc(v.tensor_sub(evlo[:, esl:esl + 512], ps[:, psl:psl + 512],
                                   evb[:, esl:esl + 512]), "sV")

        @block.gpsimd
        def _(g):
            for ot in range(32):
                esl = (ot % 2) * 512
                g.wait_ge(sems["sV"], base["sV"] + 1 + ot + 1)
                g.dma_start(out=G_FM[ot * 128:(ot + 1) * 128, t0:t0 + 512],
                            in_=evb[:, esl:esl + 512]).then_inc(sems["sS"], 16)
                g.dma_start(out=G_LO[ot * 128:(ot + 1) * 128, t0:t0 + 512],
                            in_=evlo[:, esl:esl + 512]).then_inc(sems["sS"], 16)
        P.bump("sS", 32 * 32)

        P.barrier()


def phase_final(nc, P, block, x_in, XN_TM, PROJ_FM, G_FM, G_LO, out, bufs):
    sems = P.sem
    xb, xnb, pjb, ghb, glb, gf, sf, ob = bufs
    base = P.snap()

    @block.sync
    def _(sy):
        for i in range(NT):
            sl = (i % 2) * DM
            if i >= 2:
                sy.wait_ge(sems["sV"], base["sV"] + (i - 2) * 4 + 4)
            sy.dma_start(out=xb[:, sl:sl + DM],
                         in_=x_in[i * 128:(i + 1) * 128, :]).then_inc(sems["aL"], 16)
            sy.dma_start(out=pjb[:, sl:sl + DM],
                         in_=PROJ_FM[:, i * 128:(i + 1) * 128],
                         transpose=True).then_inc(sems["tL"], 16)
            sy.dma_start(out=ghb[:, sl:sl + DM],
                         in_=G_FM[:, i * 128:(i + 1) * 128],
                         transpose=True).then_inc(sems["tL"], 16)
            sy.dma_start(out=glb[:, sl:sl + DM],
                         in_=G_LO[:, i * 128:(i + 1) * 128],
                         transpose=True).then_inc(sems["tL"], 16)
    aL0, tL0 = P.cnt["aL"], P.cnt["tL"]
    P.bump("aL", NT * 16)
    P.bump("tL", NT * 48)

    @block.scalar
    def _(s):
        for i in range(NT):
            sl = (i % 2) * DM
            if i >= 2:
                s.wait_ge(sems["sV"], base["sV"] + (i - 2) * 4 + 2)
            s.dma_start(out=xnb[:, sl:sl + DM],
                        in_=XN_TM[i * 128:(i + 1) * 128, :]).then_inc(sems["pc"], 16)
            s.wait_ge(sems["sV"], base["sV"] + i * 4 + 1)
            P.inc(s.activation(out=gf[:, sl:sl + DM], in_=gf[:, sl:sl + DM],
                               func=AF.Silu), "sA")
    pc0 = P.cnt["pc"]
    P.bump("pc", NT * 16)

    @block.vector
    def _(v):
        for i in range(NT):
            sl = (i % 2) * DM
            v.wait_ge(sems["tL"], tL0 + i * 48 + 48)
            P.inc(v.tensor_add(gf[:, sl:sl + DM], ghb[:, sl:sl + DM],
                               glb[:, sl:sl + DM]), "sV")
            v.wait_ge(sems["pc"], pc0 + (i + 1) * 16)
            P.inc(v.tensor_add(sf[:, sl:sl + DM], xnb[:, sl:sl + DM],
                               pjb[:, sl:sl + DM]), "sV")
            v.drain()
            v.wait_ge(sems["sA"], base["sA"] + i + 1)
            P.inc(v.tensor_mul(sf[:, sl:sl + DM], sf[:, sl:sl + DM],
                               gf[:, sl:sl + DM]), "sV")
            v.drain()
            v.wait_ge(sems["aL"], aL0 + (i + 1) * 16)
            if i >= 2:
                v.wait_ge(sems["sS"], base["sS"] + (i - 1) * 16)
            P.inc(v.tensor_add(ob[:, sl:sl + DM], xb[:, sl:sl + DM],
                               sf[:, sl:sl + DM]), "sV")

    @block.gpsimd
    def _(g):
        for i in range(NT):
            sl = (i % 2) * DM
            g.wait_ge(sems["sV"], base["sV"] + i * 4 + 4)
            g.dma_start(out=out[i * 128:(i + 1) * 128, :],
                        in_=ob[:, sl:sl + DM]).then_inc(sems["sS"], 16)
    P.bump("sS", NT * 16)


class _Scope:

    def __init__(self, nc):
        from contextlib import ExitStack
        self.nc = nc
        self.ctx = ExitStack()

    def sb(self, name, shape, dtype):
        return self.ctx.enter_context(self.nc.sbuf_tensor(name, shape, dtype))

    def ps(self, name, shape, dtype):
        return self.ctx.enter_context(self.nc.psum_tensor(name, shape, dtype))

    def close(self):
        self.ctx.close()


def build(taps=(), upto=None):
    nc = bass.Bass("TRN2", target_bir_lowering=False)

    def din(name, shape, dtype=F32):
        return nc.dram_tensor(name, shape, dtype, kind="ExternalInput").ap()

    x_in = din("x", [L, DM])
    WinT4 = din("WinT4", [NOT_IN, 128, NKC, 128])
    conv_w = din("conv_w", [128, NCT * CONVK])
    conv_b = din("conv_b", [128, NCT])
    dt_bias = din("dt_bias", [128, 1])
    A_log = din("A_log", [128, 1])
    Drep = din("Drep", [128, NXT])
    normw = din("normw", [128, NWO])
    WoutT = din("WoutT", [DI, DM])
    W1T4 = din("W1T4", [NOT_F1, 128, 8, 128])
    W3T4 = din("W3T4", [NOT_F1, 128, 8, 128])
    W2T2 = din("W2T2", [W2P, DM])
    a1T = din("a1T", [128, NKC * 2])
    b1T = din("b1T", [2, HF])
    a3T = din("a3T", [128, NKC * 2])
    b3T = din("b3T", [2, HF])
    a2T = din("a2T", [128, NKC_H * 2])
    b2T = din("b2T", [2, DM])

    out = nc.dram_tensor("out", [L, DM], F32, kind="ExternalOutput").ap()

    def scr(name, shape, dtype):
        kind = "ExternalOutput" if name in taps else "Internal"
        return nc.dram_tensor(name, shape, dtype, kind=kind).ap()

    XN_TM = scr("XN_TM", [L, DM], BF16)
    ZX_FM = scr("ZX_FM", [DIN, L], BF16)
    XC_FM = scr("XC_FM", [DI, L], BF16)
    XD_FM = scr("XD_FM", [DI, L], BF16)
    BC_FM = scr("BC_FM", [2 * NG * DS, L], BF16)
    CS_HI = scr("CS_HI", [128, L], BF16)
    CS_LO = scr("CS_LO", [128, L], BF16)
    YG_TM = scr("YG_TM", [L, DI], BF16)
    PROJ_FM = scr("PROJ_FM", [DM, L], BF16)
    H_FM = scr("H_FM", [HF, L], BF16)
    G_FM = scr("G_FM", [DM, L], BF16)
    G_LO = scr("G_LO", [DM, L], BF16)
    WOB = scr("WOB", [NWO, 128, DM], BF16)
    W2B = scr("W2B", [NKC_W2, 128, DM], BF16)

    from contextlib import ExitStack
    with ExitStack() as ctx:
        def sb(name, shape, dtype):
            return ctx.enter_context(nc.sbuf_tensor(name, shape, dtype))

        ident_bf = sb("ident_bf", [128, 128], BF16)
        ident_f32 = sb("ident_f32", [128, 128], F32)
        ones_bf = sb("ones_bf", [1, 512], BF16)
        ones_f32 = sb("ones_f32", [1, 512], F32)
        zeros128 = sb("zeros128", [128, 128], F32)
        m0mask = sb("m0mask", [128, 128], F32)
        vec_dtb = sb("vec_dtb", [128, 1], F32)
        vec_nA = sb("vec_nA", [128, 1], F32)
        m_convb = sb("m_convb", [128, NCT], F32)
        m_convw = sb("m_convw", [128, NCT * CONVK], F32)
        m_Drep = sb("m_Drep", [128, NXT], F32)
        m_normw = sb("m_normw", [128, NWO], F32)
        eps_col = sb("eps_col", [128, 1], F32)
        one_col = sb("one_col", [128, 1], F32)

        sems = {n: ctx.enter_context(nc.semaphore(n)) for n in SEM_NAMES}
        block = ctx.enter_context(nc.Block())
        P = Prog(nc, block, sems)

        phase_const(nc, P, block,
                    (ident_bf, ident_f32, ones_bf, ones_f32, zeros128, m0mask,
                     vec_dtb, vec_nA, m_convb, m_convw, m_Drep, m_normw,
                     eps_col, one_col),
                    (dt_bias, A_log, conv_b, conv_w, Drep, normw))
        P.barrier()

        sc = _Scope(nc)
        ln_x = sc.sb("ln_x", [128, 2 * DM], F32)
        ln_sq = sc.sb("ln_sq", [128, DM], F32)
        ln_xn = sc.sb("ln_xn", [128, 2 * DM], BF16)
        ln_st = sc.sb("ln_st", [128, 8], F32)
        pc_f32 = sc.sb("pc_f32", [128, 2 * DM], F32)
        pc_bf = sc.sb("pc_bf", [128, 2 * DM], BF16)
        phase_ln(nc, P, block, x_in, XN_TM, eps_col, (ln_x, ln_sq, ln_xn, ln_st))
        phase_precast(nc, P, block, WoutT, W2T2, WOB, W2B, m_normw,
                      (pc_f32, pc_bf))
        P.barrier()
        sc.close()

        sc = _Scope(nc)
        ip_xnf = sc.sb("ip_xnf", [128, NKC * L], BF16)
        ip_w = sc.sb("ip_w", [128, 2 * DM], BF16)
        ip_ev = sc.sb("ip_ev", [128, 2 * L], BF16)
        ip_ps = sc.ps("ip_ps", [128, 2 * L], F32)
        phase_inproj(nc, P, block, WinT4, XN_TM, ZX_FM,
                     (ip_xnf, ip_w, ip_ev), ip_ps)
        P.barrier()
        sc.close()

        sc = _Scope(nc)
        cv_xt = sc.sb("cv_xt", [128, 4112], BF16)
        cv_dg = sc.sb("cv_dg", [128, 1024], BF16)
        cv_out = sc.sb("cv_out", [128, 2 * L], BF16)
        cv_xd = sc.sb("cv_xd", [128, 2 * L], BF16)
        cv_ps = sc.ps("cv_ps", [128, 2 * L], F32)
        phase_conv(nc, P, block, ZX_FM, XC_FM, XD_FM, BC_FM, m_convw,
                   m_convb, m_Drep, ident_bf, (cv_xt, cv_dg, cv_out, cv_xd),
                   cv_ps)
        P.barrier()
        sc.close()

        rs = _Scope(nc)
        r_dt_tm = rs.sb("r_dt_tm", [128, L], BF16)
        r_expcs_tm = rs.sb("r_expcs_tm", [128, L], BF16)
        r_dexp_tm = rs.sb("r_dexp_tm", [128, L], BF16)
        r_expTrep = rs.sb("r_expTrep", [128, L], F32)
        r_ncs_hi = rs.sb("r_ncs_hi", [128, L], BF16)
        r_ncs_lo = rs.sb("r_ncs_lo", [128, L], BF16)
        r_R = rs.sb("r_R", [128, DI], F32)
        r_R16 = rs.sb("r_R16", [128, DI], BF16)

        sc = _Scope(nc)
        dtp_raw = sc.sb("dtp_raw", [128, L], BF16)
        dtp_sp1 = sc.sb("dtp_sp1", [128, L], F32)
        dtp_sp2 = sc.sb("dtp_sp2", [128, L], F32)
        dtp_cs = sc.sb("dtp_cs", [128, L], F32)
        dtp_dexp = sc.sb("dtp_dexp", [128, L], F32)
        dtp_cshi = sc.sb("dtp_cshi", [128, L], BF16)
        dtp_cslo = sc.sb("dtp_cslo", [128, L], BF16)
        dtp_trow = sc.sb("dtp_trow", [1, 128], F32)
        dtp_psb = sc.ps("dtp_psb", [128, 2048], F32)
        dtp_ps2 = sc.ps("dtp_ps2", [128, 1024], F32)
        phase_dt(nc, P, block, ZX_FM, CS_HI, CS_LO,
                 (vec_dtb, vec_nA, one_col, ident_f32, ones_f32, zeros128),
                 (r_dt_tm, r_expcs_tm, r_dexp_tm, r_expTrep, r_ncs_hi,
                  r_ncs_lo),
                 (dtp_raw, dtp_sp1, dtp_sp2, dtp_cs, dtp_dexp, dtp_cshi,
                  dtp_cslo, dtp_trow),
                 dtp_psb, dtp_ps2)
        P.barrier()
        sc.close()

        sc = _Scope(nc)
        s_x = sc.sb("s_x", [128, DI], BF16)
        s_xd = sc.sb("s_xd", [128, DI], BF16)
        s_z = sc.sb("s_z", [128, DI], BF16)
        s_xdt = sc.sb("s_xdt", [128, DI], BF16)
        s_xtl = sc.sb("s_xtl", [128, DI], BF16)
        s_m = sc.sb("s_m", [128, 2048], BF16)
        s_d = sc.sb("s_d", [128, 2048], F32)
        s_s = sc.sb("s_s", [128, 1024], BF16)
        s_btm = sc.sb("s_btm", [128, 1024], BF16)
        s_bcf = sc.sb("s_bcf", [128, 2048], BF16)
        s_tmp = sc.sb("s_tmp", [128, 1024], F32)
        s_fh = sc.sb("s_fh", [1, 4096], BF16)
        s_fl = sc.sb("s_fl", [1, 4096], BF16)
        s_rms = sc.sb("s_rms", [128, 2], F32)
        s_psA = sc.ps("s_psA", [128, 3584], F32)
        s_psT = sc.ps("s_psT", [128, 1024], BF16)
        phase_ssd(nc, P, block, XC_FM, XD_FM, ZX_FM, BC_FM, CS_HI,
                  CS_LO, YG_TM,
                  (ident_bf, ones_bf, eps_col, m0mask),
                  (r_dt_tm, r_expcs_tm, r_dexp_tm, r_expTrep, r_ncs_hi,
                   r_ncs_lo, r_R, r_R16),
                  (s_x, s_xd, s_z, s_xdt, s_xtl, s_m, s_d, s_s, s_btm,
                   s_bcf, s_tmp, s_fh, s_fl, s_rms),
                  s_psA, s_psT)
        P.barrier()
        sc.close()
        rs.close()

        sc = _Scope(nc)
        op_ygf = sc.sb("op_ygf", [128, NWO * 1024], BF16)
        op_w = sc.sb("op_w", [128, 2 * DI], BF16)
        op_ev = sc.sb("op_ev", [128, 2048], BF16)
        op_ps = sc.ps("op_ps", [128, 2048], F32)
        if upto == "ssd":
            sc.close()
            return nc
        phase_outproj(nc, P, block, YG_TM, WOB, PROJ_FM,
                      (op_ygf, op_w, op_ev), op_ps)
        sc.close()
        if upto == "outproj":
            return nc

        sc = _Scope(nc)
        f1_xnf = sc.sb("f1_xnf", [128, NKC * L], BF16)
        f1_w1 = sc.sb("f1_w1", [128, 2048], BF16)
        f1_w3 = sc.sb("f1_w3", [128, 2048], BF16)
        f1_sg = sc.sb("f1_sg", [128, 2 * L], BF16)
        f1_hv = sc.sb("f1_hv", [128, 2 * L], BF16)
        f1_a1 = sc.sb("f1_a1", [128, NKC * 2], BF16)
        f1_a3 = sc.sb("f1_a3", [128, NKC * 2], BF16)
        f1_la1 = sc.sb("f1_la1", [2, L], BF16)
        f1_la3 = sc.sb("f1_la3", [2, L], BF16)
        f1_b1 = sc.sb("f1_b1", [2, 256], BF16)
        f1_b3 = sc.sb("f1_b3", [2, 256], BF16)
        f1_psG = sc.ps("f1_psG", [128, 2048], F32)
        f1_psU = sc.ps("f1_psU", [128, 2048], F32)
        phase_ffn1(nc, P, block, XN_TM, W1T4, W3T4, a1T, a3T, b1T, b3T, H_FM,
                   (f1_xnf, f1_w1, f1_w3, f1_sg, f1_hv, f1_a1, f1_a3, f1_la1,
                    f1_la3, f1_b1, f1_b3), f1_psG, f1_psU, f1_psG)
        P.barrier()
        sc.close()
        if upto == "ffn1":
            return nc

        sc = _Scope(nc)
        f2_h = sc.sb("f2_h", [128, NKC_H * 512], BF16)
        f2_w = sc.sb("f2_w", [128, 2 * NKC_W2 * 128], BF16)
        f2_ev = sc.sb("f2_ev", [128, 1024], BF16)
        f2_evlo = sc.sb("f2_evlo", [128, 1024], BF16)
        f2_a2 = sc.sb("f2_a2", [128, NKC_H * 2], BF16)
        f2_la2 = sc.sb("f2_la2", [2, 512], BF16)
        f2_b2 = sc.sb("f2_b2", [2, DM], BF16)
        f2_ps = sc.ps("f2_ps", [128, 1024], F32)
        f2_psla = sc.ps("f2_psla", [128, 512], F32)
        phase_ffn2(nc, P, block, H_FM, W2B, a2T, b2T, G_FM, G_LO,
                   (f2_h, f2_w, f2_ev, f2_evlo, f2_a2, f2_la2, f2_b2), f2_ps,
                   f2_psla)
        sc.close()
        if upto == "ffn2":
            return nc

        sc = _Scope(nc)
        fn_x = sc.sb("fn_x", [128, 2 * DM], F32)
        fn_xn = sc.sb("fn_xn", [128, 2 * DM], BF16)
        fn_pj = sc.sb("fn_pj", [128, 2 * DM], BF16)
        fn_gh = sc.sb("fn_gh", [128, 2 * DM], BF16)
        fn_gl = sc.sb("fn_gl", [128, 2 * DM], BF16)
        fn_gf = sc.sb("fn_gf", [128, 2 * DM], F32)
        fn_sf = sc.sb("fn_sf", [128, 2 * DM], F32)
        fn_o = sc.sb("fn_o", [128, 2 * DM], F32)
        phase_final(nc, P, block, x_in, XN_TM, PROJ_FM, G_FM, G_LO, out,
                    (fn_x, fn_xn, fn_pj, fn_gh, fn_gl, fn_gf, fn_sf, fn_o))
        P.barrier()
        sc.close()

    return nc


def kernel(**inp):
    w = prep_weights(inp)
    x = inp["L0_attn_hidden_states"]
    nc = build(taps=TAPS)
    in_maps = []
    for c in range(8):
        m = {"x": _t(x[c])}
        m.update(w)
        in_maps.append(m)
    res = run_bass_kernel_spmd(nc, in_maps, core_ids=list(range(8)))
    out = np.stack([np.asarray(r["out"], dtype=np.float32) for r in res.results])
    if TAPS:
        kernel.tap_results = res.results
    return out


# version 09
# speedup vs baseline: 1.0963x; 1.0963x over previous
import numpy as np
import ml_dtypes
import concourse.bass as bass
import concourse.mybir as mybir
from concourse.bass_utils import run_bass_kernel_spmd

F32 = mybir.dt.float32
BF16 = mybir.dt.bfloat16
AF = mybir.ActivationFunctionType
ALU = mybir.AluOpType
AX = mybir.AxisListType

DM = 4096
L = 2048
DI = 8192
NH = 128
HD = 64
NG = 8
DS = 128
CONVK = 4
CDIM = DI + 2 * NG * DS
DIN = 2 * DI + 2 * NG * DS + NH
HF = 11008
EPS = 1e-5

NKC = DM // 128
NT = L // 128
NOT_IN = DIN // 128
NCT = CDIM // 128
NXT = DI // 128
NCH = L // 128
NOT_F1 = HF // 128
W2P = 5504
NKC_W2 = W2P // 128
NKC_H = HF // 128
NWO = DI // 128

TAPS = ()
FFN1_LORA = True
SSD_BARRIER = False
FFN1_PREPASS = True
FFN1_NOT = NOT_F1


def _t(x):
    return np.ascontiguousarray(x)


def prep_weights(inp):
    w = {}
    w["WinT4"] = _t(inp["L0_attn_in_proj_w"].reshape(NOT_IN, 128, NKC, 128)
                    .transpose(0, 3, 2, 1))
    w["conv_w"] = _t(inp["L0_attn_conv_w"].reshape(NCT, 128, CONVK)
                     .transpose(1, 0, 2).reshape(128, NCT * CONVK))
    w["conv_b"] = _t(inp["L0_attn_conv_b"].reshape(NCT, 128).T)
    w["dt_bias"] = _t(inp["L0_attn_dt_bias"].reshape(128, 1))
    w["A_log"] = _t(inp["L0_attn_A_log"].reshape(128, 1))
    w["Drep"] = _t(np.repeat(inp["L0_attn_D"], HD).reshape(NXT, 128).T)
    w["normw"] = _t(inp["L0_attn_norm_w"].reshape(NWO, 128).T)
    w["WoutT"] = _t(inp["L0_attn_out_proj_w"].T)
    w["W1T4"] = _t(inp["L0_ffn_w1q"].reshape(NOT_F1, 128, 8, 128).transpose(0, 3, 2, 1))
    w["W3T4"] = _t(inp["L0_ffn_w3q"].reshape(NOT_F1, 128, 8, 128).transpose(0, 3, 2, 1))
    w2t = inp["L0_ffn_w2q"].T
    w["W2T2"] = _t(np.concatenate([w2t, w2t], axis=0))
    w["a1T"] = _t(inp["L0_ffn_a1"].T.reshape(NKC, 128, 2)
                  .transpose(1, 0, 2).reshape(128, NKC * 2))
    w["b1T"] = _t(inp["L0_ffn_b1"].T)
    w["a3T"] = _t(inp["L0_ffn_a3"].T.reshape(NKC, 128, 2)
                  .transpose(1, 0, 2).reshape(128, NKC * 2))
    w["b3T"] = _t(inp["L0_ffn_b3"].T)
    w["a2T"] = _t(inp["L0_ffn_a2"].T.reshape(NKC_H, 128, 2)
                  .transpose(1, 0, 2).reshape(128, NKC_H * 2))
    w["b2T"] = _t(inp["L0_ffn_b2"].T)
    return w


SEM_NAMES = ["wL", "wL2", "aL", "aL2", "tL", "tL2", "sS", "sS2", "pc", "pc2",
             "pS", "pcC", "aF0", "aF1", "tX", "tD", "tZ",
             "sT", "sV", "sA", "sG", "bar"]


class Prog:

    def __init__(self, nc, block, sems):
        self.nc = nc
        self.block = block
        self.sem = sems
        self.cnt = {k: 0 for k in sems}
        self.nbar = 0

    def inc(self, ins, name, n=1):
        ins.then_inc(self.sem[name], n)
        self.cnt[name] += n
        return self.cnt[name]

    def bump(self, name, n):
        self.cnt[name] += n

    def snap(self):
        return dict(self.cnt)

    def barrier(self):
        snap = self.snap()
        self.nbar += 1
        target = 5 * self.nbar
        for sec in (self.block.tensor, self.block.vector, self.block.scalar,
                    self.block.gpsimd, self.block.sync):
            def body(eng, snap=snap, target=target):
                for k, v in snap.items():
                    if k != "bar" and v > 0:
                        eng.wait_ge(self.sem[k], v)
                eng.sem_inc(self.sem["bar"], 1)
                eng.wait_ge(self.sem["bar"], target)
            sec(body)
        self.cnt["bar"] = target



def phase_const(nc, P, block, bufs, ins):
    sems = P.sem
    (ident_bf, ident_f32, ones_bf, ones_f32, zeros128, m0mask, vec_dtb, vec_nA,
     m_convb, m_convw, m_Drep, m_normw, eps_col, one_col) = bufs
    dt_bias, A_log, conv_b, conv_w, Drep, normw = ins

    @block.gpsimd
    def _(g):
        g.dma_start(out=vec_dtb[:], in_=dt_bias).then_inc(sems["aL"], 16)
        g.dma_start(out=vec_nA[:], in_=A_log).then_inc(sems["aL"], 16)
        g.dma_start(out=m_convb[:], in_=conv_b).then_inc(sems["aL"], 16)
        g.dma_start(out=m_convw[:], in_=conv_w).then_inc(sems["aL"], 16)
        g.dma_start(out=m_Drep[:], in_=Drep).then_inc(sems["aL"], 16)
        g.dma_start(out=m_normw[:], in_=normw).then_inc(sems["aL"], 16)
    P.bump("aL", 6 * 16)

    @block.vector
    def _(v):
        v.memset(zeros128[:], 0.0)
        v.memset(ones_bf[:], 1.0)
        v.memset(ones_f32[:], 1.0)
        v.memset(eps_col[:], EPS)
        v.memset(one_col[:], 1.0)
        P.inc(v.memset(ident_f32[:], 1.0), "sV")

    @block.gpsimd
    def _(g):
        g.wait_ge(sems["sV"], P.cnt["sV"])
        g.affine_select(out=ident_f32[:], in_=ident_f32[:], pattern=[[1, 128]],
                        compare_op=ALU.is_equal, fill=0.0, base=0,
                        channel_multiplier=-1)
        P.inc(g.affine_select(out=m0mask[:], in_=zeros128[:], pattern=[[1, 128]],
                              compare_op=ALU.is_ge, fill=-1e30, base=0,
                              channel_multiplier=-1), "sG")

    @block.vector
    def _(v):
        v.wait_ge(sems["sG"], P.cnt["sG"])
        P.inc(v.tensor_copy(ident_bf[:], ident_f32[:]), "sV")

    @block.scalar
    def _(s):
        s.wait_ge(sems["aL"], P.cnt["aL"])
        P.inc(s.activation(out=vec_nA[:], in_=vec_nA[:], func=AF.Exp, scale=1.0), "sA")

    @block.vector
    def _(v):
        v.wait_ge(sems["sA"], P.cnt["sA"])
        P.inc(v.tensor_scalar_mul(vec_nA[:], vec_nA[:], -1.0), "sV")


def phase_ln(nc, P, block, x_in, XN_TM, eps_col, ident_bf, bufs, ip_ps):
    sems = P.sem
    xbuf, xnbuf, st, xnf, sqd = bufs
    psT = ip_ps[:, 0:2048].bitcast(BF16)
    base = P.snap()

    @block.sync
    def _(sy):
        for i in range(NT):
            sl = (i % 2) * DM
            sem_i = sems["aL"] if i % 2 == 0 else sems["aL2"]
            if i >= 2:
                sy.wait_ge(sems["sV"], base["sV"] + (i - 2) * 4 + 3)
                sy.wait_ge(sems["sA"], base["sA"] + (i - 2) * 2 + 1)
            sy.dma_start(out=xbuf[:, sl:sl + DM],
                         in_=x_in[i * 128:(i + 1) * 128, :]).then_inc(sem_i, 16)
    aL0, aL20 = P.cnt["aL"], P.cnt["aL2"]
    P.bump("aL", (NT // 2) * 16)
    P.bump("aL2", (NT // 2) * 16)

    def _ln_ld(i):
        return (sems["aL"] if i % 2 == 0 else sems["aL2"],
                (aL0 if i % 2 == 0 else aL20) + (i // 2 + 1) * 16)

    @block.vector
    def _(v):
        for i in range(NT):
            sl = (i % 2) * DM
            sq = 4 + (i % 2) * 3
            v.wait_ge(*_ln_ld(i))
            xv = xbuf[:, sl:sl + DM]
            v.reduce_sum(st[:, 0:1], xv, axis=AX.X)
            v.drain()
            v.tensor_scalar_mul(st[:, 1:2], st[:, 0:1], 1.0 / DM)
            v.drain()
            P.inc(v.tensor_mul(st[:, 2:3], st[:, 0:1], st[:, 1:2]), "sV")
            v.drain()
            v.wait_ge(sems["sA"], base["sA"] + i * 2 + 1)
            P.inc(v.tensor_sub(st[:, 3:4], st[:, sq:sq + 1], st[:, 2:3]), "sV")
            v.wait_ge(sems["sA"], base["sA"] + i * 2 + 2)
            if i >= 1:
                v.wait_ge(sems["sS"], base["sS"] + i * 16)
                v.wait_ge(sems["sT"], base["sT"] + i)
            v.tensor_scalar(xnbuf[:], xv, st[:, 1:2], st[:, 6:7],
                            ALU.subtract, ALU.mult)
            P.inc(v.drain(), "sV")
            if i >= 1:
                v.wait_ge(sems["sT"], base["sT"] + i)
                P.inc(v.tensor_copy(
                    xnf[:].rearrange("p (kc t) -> p kc t", kc=NKC)
                    [:, :, (i - 1) * 128:i * 128],
                    psT[:].rearrange("p (kc t) -> p kc t", kc=NKC)), "sV")
            else:
                P.inc(v.drain(), "sV")
        v.wait_ge(sems["sT"], base["sT"] + NT)
        P.inc(v.tensor_copy(
            xnf[:].rearrange("p (kc t) -> p kc t", kc=NKC)
            [:, :, (NT - 1) * 128:NT * 128],
            psT[:].rearrange("p (kc t) -> p kc t", kc=NKC)), "sV")

    @block.scalar
    def _(s):
        for i in range(NT):
            sl = (i % 2) * DM
            sq = 4 + (i % 2) * 3
            s.wait_ge(*_ln_ld(i))
            P.inc(s.activation(out=sqd[:], in_=xbuf[:, sl:sl + DM],
                               func=AF.Square, accum_out=st[:, sq:sq + 1]), "sA")
            s.wait_ge(sems["sV"], base["sV"] + i * 4 + 2)
            s.activation(out=st[:, 5:6], in_=st[:, 3:4], func=AF.Ln,
                         scale=1.0 / DM, bias=eps_col[:])
            P.inc(s.activation(out=st[:, 6:7], in_=st[:, 5:6],
                               func=AF.Exp, scale=-0.5), "sA")
            s.wait_ge(sems["sV"], base["sV"] + i * 4 + 3)
            s.dma_start(out=XN_TM[i * 128:(i + 1) * 128, :],
                        in_=xnbuf[:]).then_inc(sems["sS"], 16)
    P.bump("sS", NT * 16)

    @block.tensor
    def _(t):
        for i in range(NT):
            if i >= 1:
                t.wait_ge(sems["sV"], base["sV"] + i * 4 + 4)
            else:
                t.wait_ge(sems["sV"], base["sV"] + 3)
            last = None
            for kc in range(NKC):
                last = t.transpose(psT[:, kc * 128:(kc + 1) * 128],
                                   xnbuf[:, kc * 128:(kc + 1) * 128], ident_bf[:])
            P.inc(last, "sT")

    return P.cnt["sV"]


def phase_precast(nc, P, block, WoutT, W2T2, WOB, W2B, m_normw, bufs):
    sems = P.sem
    NTOT = NWO + NKC_W2
    f32buf, bfbuf = bufs
    if True:
        base = P.snap()

        @block.sync
        def _(sy):
            for i in range(NTOT):
                sl = (i % 2) * DM
                if i >= 2:
                    sy.wait_ge(sems["sG"], base["sG"] + (i - 2) + 1)
                src = (WoutT[i * 128:(i + 1) * 128, :] if i < NWO
                       else W2T2[(i - NWO) * 128:(i - NWO + 1) * 128, :])
                sy.dma_start(out=f32buf[:, sl:sl + DM], in_=src).then_inc(sems["pc"], 16)
        pc0 = P.cnt["pc"]
        P.bump("pc", NTOT * 16)

        @block.gpsimd
        def _(g):
            for i in range(NTOT):
                sl = (i % 2) * DM
                g.wait_ge(sems["pc"], pc0 + (i + 1) * 16)
                if i >= 2:
                    g.wait_ge(sems["pS"], base["pS"] + (i - 1) * 16)
                if i < NWO:
                    ins = g.tensor_scalar(bfbuf[:, sl:sl + DM], f32buf[:, sl:sl + DM],
                                          m_normw[:, i:i + 1], None, ALU.mult)
                else:
                    ins = g.tensor_copy(bfbuf[:, sl:sl + DM], f32buf[:, sl:sl + DM])
                P.inc(ins, "sG")

        @block.scalar
        def _(s):
            for i in range(NTOT):
                sl = (i % 2) * DM
                s.wait_ge(sems["sG"], base["sG"] + i + 1)
                dst = WOB[i] if i < NWO else W2B[i - NWO]
                s.dma_start(out=dst, in_=bfbuf[:, sl:sl + DM]).then_inc(sems["pS"], 16)
        P.bump("pS", NTOT * 16)


def phase_inproj(nc, P, block, WinT4, ZX_FM, bufs, ps, xnf_ready):
    sems = P.sem
    xnf, wbuf, evb = bufs
    base = P.snap()

    @block.gpsimd
    def _(g):
        for ot in range(NOT_IN):
            sl = (ot % 2) * DM
            if ot >= 2:
                g.wait_ge(sems["sT"], base["sT"] + (ot - 2) + 1)
            g.dma_start(out=wbuf[:, sl:sl + DM], in_=WinT4[ot]).then_inc(
                sems["wL"] if ot % 2 == 0 else sems["wL2"], 16)
    wL0, wL20 = P.cnt["wL"], P.cnt["wL2"]
    P.bump("wL", ((NOT_IN + 1) // 2) * 16)
    P.bump("wL2", (NOT_IN // 2) * 16)

    aA0 = P.cnt["sA"]

    @block.tensor
    def _(t):
        t.wait_ge(sems["sV"], xnf_ready)
        t.wait_ge(sems["sA"], aA0)
        for ot in range(NOT_IN):
            sl = (ot % 2) * DM
            pslot = (ot % 2) * L
            t.wait_ge(sems["wL"] if ot % 2 == 0 else sems["wL2"],
                      (wL0 if ot % 2 == 0 else wL20) + (ot // 2 + 1) * 16)
            if ot >= 2:
                t.wait_ge(sems["sV"], base["sV"] + (ot - 2) + 1)
            last = None
            for kc in range(NKC):
                lhsT = wbuf[:, sl + kc * 128: sl + (kc + 1) * 128]
                for tt in range(4):
                    last = t.matmul(ps[:, pslot + tt * 512: pslot + (tt + 1) * 512],
                                    lhsT, xnf[:, kc * L + tt * 512: kc * L + (tt + 1) * 512],
                                    start=(kc == 0), stop=(kc == NKC - 1))
            P.inc(last, "sT")

    sS0, sS20 = P.cnt["sS"], P.cnt["sS2"]

    @block.vector
    def _(v):
        for ot in range(NOT_IN):
            sl = (ot % 2) * L
            v.wait_ge(sems["sT"], base["sT"] + ot + 1)
            if ot >= 2:
                v.wait_ge(sems["sS"] if ot % 2 == 0 else sems["sS2"],
                          (sS0 if ot % 2 == 0 else sS20) + ((ot - 2) // 2 + 1) * 16)
            P.inc(v.tensor_copy(evb[:, sl:sl + L], ps[:, sl:sl + L]), "sV")

    @block.scalar
    def _(s):
        for ot in range(NOT_IN):
            sl = (ot % 2) * L
            s.wait_ge(sems["sV"], base["sV"] + ot + 1)
            s.dma_start(out=ZX_FM[ot * 128:(ot + 1) * 128, :],
                        in_=evb[:, sl:sl + L]).then_inc(
                sems["sS"] if ot % 2 == 0 else sems["sS2"], 16)
    P.bump("sS", ((NOT_IN + 1) // 2) * 16)
    P.bump("sS2", (NOT_IN // 2) * 16)


def phase_conv(nc, P, block, ZX_FM, XC_FM, XD_FM, BC_FM, m_convw, m_convb,
               m_Drep, ident_bf, bufs, ps):
    sems = P.sem
    xt, dg, cvb, xdb = bufs
    base = P.snap()
    cS0, cS20 = P.cnt["sS"], P.cnt["sS2"]

    @block.vector
    def _(v):
        v.memset(xt[:, 0:3], 0.0)
        P.inc(v.memset(xt[:, 2054:2057], 0.0), "sV")

    @block.gpsimd
    def _(g):
        for ct in range(NCT):
            sl = (ct % 2) * 2054
            if ct >= 2:
                g.wait_ge(sems["sT"], base["sT"] + (ct - 2) + 1)
            g.dma_start(out=xt[:, sl + 3: sl + 3 + L],
                        in_=ZX_FM[DI + ct * 128: DI + (ct + 1) * 128, :]
                        ).then_inc(sems["aL"] if ct % 2 == 0 else sems["aL2"], 16)
    aL0, aL20 = P.cnt["aL"], P.cnt["aL2"]
    P.bump("aL", (NCT // 2) * 16)
    P.bump("aL2", (NCT // 2) * 16)

    @block.vector
    def _(v):
        for ct in range(NCT + 2):
            if ct < NCT:
                dsl = (ct % 2) * 512
                if ct >= 2:
                    v.wait_ge(sems["sT"], base["sT"] + (ct - 2) + 1)
                for k in range(CONVK):
                    v.tensor_scalar(dg[:, dsl + k * 128: dsl + (k + 1) * 128],
                                    ident_bf[:],
                                    m_convw[:, ct * CONVK + k: ct * CONVK + k + 1],
                                    None, ALU.mult)
                P.inc(v.drain(), "sG")
            ct2 = ct - 2
            if 0 <= ct2 < NXT:
                sl = (ct2 % 2) * L
                v.wait_ge(sems["sA"], base["sA"] + ct2 + 1)
                if ct2 >= 2:
                    v.wait_ge(sems["sS"] if ct2 % 2 == 0 else sems["sS2"],
                              (cS0 if ct2 % 2 == 0 else cS20)
                              + _conv_nstores_par(ct2 - 1, ct2 % 2) * 16)
                v.tensor_scalar(xdb[:, sl:sl + L], cvb[:, sl:sl + L],
                                m_Drep[:, ct2:ct2 + 1], None, ALU.mult)
                P.inc(v.drain(), "sV")

    @block.tensor
    def _(t):
        t.wait_ge(sems["sV"], base["sV"] + 1)
        for ct in range(NCT):
            sl = (ct % 2) * 2054
            dsl = (ct % 2) * 512
            psl = (ct % 2) * L
            t.wait_ge(sems["aL"] if ct % 2 == 0 else sems["aL2"],
                      (aL0 if ct % 2 == 0 else aL20) + (ct // 2 + 1) * 16)
            t.wait_ge(sems["sG"], base["sG"] + ct + 1)
            if ct >= 2:
                t.wait_ge(sems["sA"], base["sA"] + (ct - 2) + 1)
            last = None
            for k in range(CONVK):
                lhsT = dg[:, dsl + k * 128: dsl + (k + 1) * 128]
                for tt in range(4):
                    last = t.matmul(ps[:, psl + tt * 512: psl + (tt + 1) * 512],
                                    lhsT,
                                    xt[:, sl + k + tt * 512: sl + k + tt * 512 + 512],
                                    start=(k == 0), stop=(k == CONVK - 1))
            P.inc(last, "sT")

    @block.scalar
    def _(s):
        for ct in range(NCT):
            sl = (ct % 2) * L
            s.wait_ge(sems["sT"], base["sT"] + ct + 1)
            if ct >= 2:
                s.wait_ge(sems["sS"] if ct % 2 == 0 else sems["sS2"],
                          (cS0 if ct % 2 == 0 else cS20)
                          + _conv_nstores_par(ct - 1, ct % 2) * 16)
            s.activation(out=cvb[:, sl:sl + L], in_=ps[:, sl:sl + L],
                         func=AF.Silu, bias=m_convb[:, ct:ct + 1], scale=1.0)
            P.inc(s.drain(), "sA")

    @block.sync
    def _(sy):
        for ct in range(NCT):
            sl = (ct % 2) * L
            ssem = sems["sS"] if ct % 2 == 0 else sems["sS2"]
            sy.wait_ge(sems["sA"], base["sA"] + ct + 1)
            if ct < NXT:
                sy.wait_ge(sems["sV"], base["sV"] + 1 + ct + 1)
                sy.dma_start(out=XC_FM[ct * 128:(ct + 1) * 128, :],
                             in_=cvb[:, sl:sl + L]).then_inc(ssem, 16)
                sy.dma_start(out=XD_FM[ct * 128:(ct + 1) * 128, :],
                             in_=xdb[:, sl:sl + L]).then_inc(ssem, 16)
            else:
                sy.dma_start(out=BC_FM[(ct - NXT) * 128:(ct - NXT + 1) * 128, :],
                             in_=cvb[:, sl:sl + L]).then_inc(ssem, 16)
    P.bump("sS", _conv_nstores_par(NCT, 0) * 16)
    P.bump("sS2", _conv_nstores_par(NCT, 1) * 16)


def _conv_nstores(ct):
    return 2 * min(ct, NXT) + max(ct - NXT, 0)


def _conv_nstores_par(ct, par):
    n = 0
    for c in range(par, ct, 2):
        n += 2 if c < NXT else 1
    return n


def phase_dt(nc, P, block, ZX_FM, CS_HI, CS_LO, consts, res, scratch, psb, ps2):
    sems = P.sem
    (vec_dtb, vec_nA, one_col, ident_f32, ones_f32, zeros128) = consts
    (dt_tm, expcs_tm, dexp_tm, expTrep, ncs_hi, ncs_lo) = res
    (dtraw, sp1, sp2, cs_fm, dexp_fm, cshi, cslo, trow) = scratch
    base = P.snap()

    @block.gpsimd
    def _(g):
        g.dma_start(out=dtraw[:], in_=ZX_FM[DIN - 128:DIN, :]).then_inc(sems["aL"], 16)
    aL0 = P.cnt["aL"]
    P.bump("aL", 16)

    @block.scalar
    def _(s):
        s.wait_ge(sems["aL"], aL0 + 16)
        s.activation(out=sp1[:], in_=dtraw[:], func=AF.Relu, bias=vec_dtb[:])
        s.activation(out=sp2[:], in_=dtraw[:], func=AF.Abs, bias=vec_dtb[:])
        s.activation(out=sp2[:], in_=sp2[:], func=AF.Exp, scale=-1.0)
        P.inc(s.activation(out=sp2[:], in_=sp2[:], func=AF.Ln, bias=one_col[:]), "sA")

    @block.vector
    def _(v):
        v.wait_ge(sems["sA"], base["sA"] + 1)
        v.tensor_add(sp1[:], sp1[:], sp2[:])
        v.drain()
        v.tensor_scalar(sp2[:], sp1[:], vec_nA[:], None, ALU.mult)
        v.drain()
        for c in range(NCH):
            v.tensor_tensor_scan(out=cs_fm[:, c * 128:(c + 1) * 128],
                                 data0=sp2[:, c * 128:(c + 1) * 128],
                                 data1=zeros128[:], initial=0.0,
                                 op0=ALU.add, op1=ALU.add)
        v.drain()
        v.tensor_copy(cshi[:], cs_fm[:])
        v.drain()
        v.tensor_sub(cslo[:], cs_fm[:], cshi[:])
        v.drain()
        v.tensor_scalar_mul(ncs_hi[:], cshi[:], -1.0)
        P.inc(v.tensor_scalar_mul(ncs_lo[:], cslo[:], -1.0), "sV")

    @block.sync
    def _(sy):
        sy.wait_ge(sems["sV"], base["sV"] + 1)
        sy.dma_start(out=CS_HI, in_=cshi[:]).then_inc(sems["sS"], 16)
        sy.dma_start(out=CS_LO, in_=cslo[:]).then_inc(sems["sS"], 16)
    P.bump("sS", 2 * 16)

    @block.scalar
    def _(s):
        s.wait_ge(sems["sV"], base["sV"] + 1)
        for c in range(NCH):
            ins = s.activation(out=dexp_fm[:, c * 128:(c + 1) * 128],
                               in_=cs_fm[:, c * 128:(c + 1) * 128], func=AF.Exp,
                               scale=-1.0,
                               bias=cs_fm[:, c * 128 + 127:c * 128 + 128])
            P.inc(ins, "sA")

    vb = P.cnt["sV"]
    ab = P.cnt["sA"]
    tb = P.cnt["sT"]

    @block.tensor
    def _(t):
        for c in range(NCH):
            t.wait_ge(sems["sA"], ab - NCH + c + 1)
            if c >= 1:
                t.wait_ge(sems["sV"], vb + c)
                t.wait_ge(sems["sA"], ab + c)
            cs_c = cs_fm[:, c * 128:(c + 1) * 128]
            t.transpose(psb[:, 0:128], sp1[:, c * 128:(c + 1) * 128], ident_f32[:])
            t.transpose(psb[:, 512:640], cs_c, ident_f32[:])
            t.transpose(psb[:, 1024:1152], dexp_fm[:, c * 128:(c + 1) * 128],
                        ident_f32[:])
            P.inc(t.transpose(psb[0:1, 1536:1664],
                              cs_fm[:, c * 128 + 127:c * 128 + 128],
                              ident_f32[:]), "sT")
            t.wait_ge(sems["sV"], vb + c + 1)
            P.inc(t.matmul(ps2[:, (c % 2) * 512:(c % 2) * 512 + 128],
                           ones_f32[0:1, 0:128], trow[0:1, :],
                           start=True, stop=True), "sT")

    @block.vector
    def _(v):
        for c in range(NCH):
            v.wait_ge(sems["sT"], tb + 2 * c + 1)
            v.tensor_copy(dt_tm[:, c * 128:(c + 1) * 128], psb[:, 0:128])
            v.tensor_copy(dexp_tm[:, c * 128:(c + 1) * 128], psb[:, 1024:1152])
            P.inc(v.tensor_copy(trow[0:1, :], psb[0:1, 1536:1664]), "sV")

    @block.scalar
    def _(s):
        for c in range(NCH):
            s.wait_ge(sems["sT"], tb + 2 * c + 1)
            s.activation(out=expcs_tm[:, c * 128:(c + 1) * 128],
                         in_=psb[:, 512:640], func=AF.Exp)
            s.wait_ge(sems["sT"], tb + 2 * c + 2)
            P.inc(s.activation(out=expTrep[:, c * 128:(c + 1) * 128],
                               in_=ps2[:, (c % 2) * 512:(c % 2) * 512 + 128],
                               func=AF.Exp), "sA")
    P.bump("sA", 0)


def phase_ssd(nc, P, block, XC_FM, XD_FM, ZX_FM, BC_FM, CS_HI, CS_LO, YG_TM,
              consts, res, bufs, psA, psT):
    sems = P.sem
    (ident_bf, ones_bf, eps_col, m0mask) = consts
    (dt_bf, expcs_tm, dexp_bf, expTrep, ncs_hi, ncs_lo, R, R16) = res
    (sx, sxd, sz, sxdt, sxtl, sm, sd, ss_sb, sbtm, sbcf, stmp, sfh, sfl,
     srms) = bufs

    @block.vector
    def _(v):
        v.memset(R[:], 0.0)
        P.inc(v.memset(R16[:], 0.0), "sV")

    base = P.snap()
    tb, vb, ab, gb, lb, fb, sb = (base["sT"], base["sV"], base["sA"],
                                  base["sG"], base["tL"], base["aL"],
                                  base["sS"])
    lb_x, lb_d, lb_z = base["tX"], base["tD"], base["tZ"]
    TPC, VPC, APC, GPC, LPC, FPC, SPC = 34, 60, 11, 10, 48, 288, 16

    def tOf(c, off=0):
        return tb + TPC * c + off

    def vOf(c, off=0):
        return vb + VPC * c + off

    def aAOf(c, off=0):
        return ab + APC * c + off

    def gOf(c, off=0):
        return gb + GPC * c + off

    def lOf(c, off=0):
        return lb + LPC * c + off

    def aOf(c, off=0):
        return fb + 32 * c + off

    def fOf(c, gg):
        f0 = base["aF0"] if gg % 2 == 0 else base["aF1"]
        return f0 + (c * 4 + gg // 2 + 1) * 32

    psS = psA[:, 0:1024]
    psY1 = psA[:, 0:1024]
    psY2 = psA[:, 1024:2048]
    psDel = psA[:, 2048:3072]

    def emit_sync(c):
        @block.sync
        def _(sy):
            if c >= 1:
                sy.wait_ge(sems["sV"], vOf(c - 1, 60))
                sy.dma_start(out=YG_TM[(c - 1) * 128:c * 128, :],
                             in_=sx[:]).then_inc(sems["sS"], 16)
            sy.wait_ge(sems["sS"], sb + 16 * c)
            sy.dma_start(out=sx[:], in_=XC_FM[:, c * 128:(c + 1) * 128],
                         transpose=True).then_inc(sems["tX"], 16)
            if c >= 1:
                sy.wait_ge(sems["sV"], vOf(c - 1, 55))
            sy.dma_start(out=sxd[:], in_=XD_FM[:, c * 128:(c + 1) * 128],
                         transpose=True).then_inc(sems["tD"], 16)
            if c >= 1:
                sy.wait_ge(sems["sV"], vOf(c - 1, 59))
            sy.dma_start(out=sz[:], in_=ZX_FM[0:DI, c * 128:(c + 1) * 128],
                         transpose=True).then_inc(sems["tZ"], 16)
            sy.wait_ge(sems["sT"], tOf(c, 0))
            bcs = BC_FM[:, c * 128:(c + 1) * 128]
            sy.dma_start(out=sbcf[:].rearrange("p (g t) -> p g t", g=16)[:, 0:8, :],
                         in_=bcs[0:1024].rearrange("(g p) t -> p g t", p=128)
                         ).then_inc(sems["aL"], 16)
            sy.dma_start(out=sbcf[:].rearrange("p (g t) -> p g t", g=16)[:, 8:16, :],
                         in_=bcs[1024:2048].rearrange("(g p) t -> p g t", p=128)
                         ).then_inc(sems["aL"], 16)
            for gg in range(8):
                if gg >= 2:
                    sy.wait_ge(sems["sT"], tOf(c, 3 + 4 * (gg - 2)))
                fsl = (gg % 2) * 2048
                fsem = sems["aF0"] if gg % 2 == 0 else sems["aF1"]
                sy.dma_start(out=sfh[:, fsl:fsl + 2048]
                             .rearrange("o (h t) -> o h t", h=16),
                             in_=CS_HI[16 * gg:16 * gg + 16, c * 128:(c + 1) * 128]
                             .unsqueeze(0)).then_inc(fsem, 16)
                sy.dma_start(out=sfl[:, fsl:fsl + 2048]
                             .rearrange("o (h t) -> o h t", h=16),
                             in_=CS_LO[16 * gg:16 * gg + 16, c * 128:(c + 1) * 128]
                             .unsqueeze(0)).then_inc(fsem, 16)
        P.bump("tX", 16)
        P.bump("tD", 16)
        P.bump("tZ", 16)
        P.bump("aL", 32)
        P.bump("aF0", 128)
        P.bump("aF1", 128)
        if c >= 1:
            P.bump("sS", SPC)

    def emit_final_store():
        @block.sync
        def _(sy):
            sy.wait_ge(sems["sV"], vOf(NCH - 1, 60))
            sy.dma_start(out=YG_TM[(NCH - 1) * 128:NCH * 128, :],
                         in_=sx[:]).then_inc(sems["sS"], 16)
        P.bump("sS", SPC)

    def emit_tensor(c):
        @block.tensor
        def _(t):
            if c >= 1:
                t.wait_ge(sems["sV"], vOf(c - 1, 58))
                t.wait_ge(sems["sG"], gOf(c - 1, 10))
            else:
                t.wait_ge(sems["sV"], vb)
            t.wait_ge(sems["aL"], aOf(c, 32))
            last = None
            for g in range(8):
                last = t.transpose(psT[:, g * 128:(g + 1) * 128],
                                   sbcf[:, g * 128:(g + 1) * 128], ident_bf[:])
            P.inc(last, "sT")
            last = None
            for g in range(8):
                last = t.matmul(psS[:, g * 128:(g + 1) * 128],
                                sbcf[:, g * 128:(g + 1) * 128],
                                sbcf[:, (8 + g) * 128:(9 + g) * 128],
                                start=True, stop=True)
            P.inc(last, "sT")
            for gg in range(8):
                t.wait_ge(sems["aF0"] if gg % 2 == 0 else sems["aF1"], fOf(c, gg))
                if gg >= 1:
                    t.wait_ge(sems["sV"], vOf(c, 9 + 7 * (gg - 1)))
                fsl = (gg % 2) * 2048
                last = None
                for q in range(4):
                    dsl = 1536 + q * 512
                    identq = (ident_bf[:, 16 * gg + 4 * q:16 * gg + 4 * q + 4]
                              .unsqueeze(2).broadcast_to([128, 4, 128]))
                    ncol = c * 128
                    t.matmul(psA[:, dsl:dsl + 512],
                             ncs_hi[:, ncol:ncol + 128], identq,
                             start=True, stop=False)
                    t.matmul(psA[:, dsl:dsl + 512],
                             ncs_lo[:, ncol:ncol + 128], identq,
                             start=False, stop=False)
                    t.matmul(psA[:, dsl:dsl + 512],
                             ones_bf[0:1, 0:128],
                             sfh[0:1, fsl + q * 512:fsl + (q + 1) * 512],
                             start=False, stop=False)
                    last = t.matmul(psA[:, dsl:dsl + 512],
                                    ones_bf[0:1, 0:128],
                                    sfl[0:1, fsl + q * 512:fsl + (q + 1) * 512],
                                    start=False, stop=True)
                P.inc(last, "sT")
                t.wait_ge(sems["sV"], vOf(c, 3 + 7 * gg))
                cfm = sbcf[:, (8 + gg) * 128:(9 + gg) * 128]
                last = None
                for hh in range(16):
                    h = 16 * gg + hh
                    last = t.matmul(psY2[:, hh * 64:(hh + 1) * 64], cfm,
                                    R16[:, h * 64:(h + 1) * 64],
                                    start=True, stop=True)
                P.inc(last, "sT")
                t.wait_ge(sems["sV"], vOf(c, 4 + 7 * gg))
                if gg >= 1:
                    t.wait_ge(sems["sV"], vOf(c, 7 + 7 * (gg - 1)))
                last = None
                for hh in range(16):
                    h = 16 * gg + hh
                    last = t.matmul(psY1[:, hh * 64:(hh + 1) * 64],
                                    sm[:, hh * 128:(hh + 1) * 128],
                                    sxdt[:, h * 64:(h + 1) * 64],
                                    start=True, stop=True)
                P.inc(last, "sT")
                if gg == 0:
                    t.wait_ge(sems["sG"], gOf(c, 2))
                    t.wait_ge(sems["sV"], vOf(c, 1))
                last = None
                for hh in range(16):
                    h = 16 * gg + hh
                    last = t.matmul(psDel[:, hh * 64:(hh + 1) * 64],
                                    sbtm[:, gg * 128:(gg + 1) * 128],
                                    sxtl[:, h * 64:(h + 1) * 64],
                                    start=True, stop=True)
                P.inc(last, "sT")

    def emit_vector(c):
        @block.vector
        def _(v):
            v.wait_ge(sems["sT"], tOf(c, 1))
            for g in range(8):
                v.tensor_copy(sbtm[:, g * 128:(g + 1) * 128],
                              psT[:, g * 128:(g + 1) * 128])
            P.inc(v.drain(), "sV")
            v.wait_ge(sems["sT"], tOf(c, 2))
            v.tensor_copy(ss_sb[:], psS[:])
            P.inc(v.drain(), "sV")
            for gg in range(8):
                v.wait_ge(sems["sT"], tOf(c, 3 + 4 * gg))
                if gg >= 1:
                    v.wait_ge(sems["sA"], aAOf(c, gg))
                v.tensor_add(sd[:].rearrange("p (a b) -> p a b", a=16),
                             psA[:, 1536:3584].rearrange("p (a b) -> p a b", a=16),
                             m0mask[:].unsqueeze(1).broadcast_to([128, 16, 128]))
                P.inc(v.drain(), "sV")
                v.wait_ge(sems["sA"], aAOf(c, 1 + gg))
                if gg >= 1:
                    v.wait_ge(sems["sT"], tOf(c, 5 + 4 * (gg - 1)))
                v.tensor_mul(sm[:].rearrange("p (a b) -> p a b", a=16),
                             sd[:].rearrange("p (a b) -> p a b", a=16),
                             ss_sb[:, gg * 128:(gg + 1) * 128].unsqueeze(1)
                             .broadcast_to([128, 16, 128]))
                P.inc(v.drain(), "sV")
                v.wait_ge(sems["sT"], tOf(c, 4 + 4 * gg))
                ecol = c * 128 + 16 * gg
                P.inc(v.tensor_mul(stmp[:].rearrange("p (a b) -> p a b", a=16),
                                   psY2[:].rearrange("p (a b) -> p a b", a=16),
                                   expcs_tm[:, ecol:ecol + 16].unsqueeze(2)
                                   .broadcast_to([128, 16, 64])), "sV")
                v.drain()
                if gg == 0:
                    v.wait_ge(sems["tD"], lb_d + (c + 1) * 16)
                P.inc(v.tensor_add(stmp[:], stmp[:],
                                   sxd[:, gg * 1024:(gg + 1) * 1024]), "sV")
                v.drain()
                v.wait_ge(sems["sT"], tOf(c, 5 + 4 * gg))
                P.inc(v.tensor_add(sxdt[:, gg * 1024:(gg + 1) * 1024],
                                   stmp[:], psY1[:]), "sV")
                v.drain()
                P.inc(v.tensor_mul(stmp[:].rearrange("p (a b) -> p a b", a=16),
                                   R[:, gg * 1024:(gg + 1) * 1024]
                                   .rearrange("p (a b) -> p a b", a=16),
                                   expTrep[:, ecol:ecol + 16].unsqueeze(2)
                                   .broadcast_to([128, 16, 64])), "sV")
                v.drain()
                v.wait_ge(sems["sT"], tOf(c, 6 + 4 * gg))
                v.tensor_add(R[:, gg * 1024:(gg + 1) * 1024],
                             stmp[:], psDel[:])
                P.inc(v.drain(), "sV")
            v.wait_ge(sems["sA"], aAOf(c, 9))
            v.wait_ge(sems["sT"], tOf(c, 34))
            v.tensor_mul(sxtl[:], sxdt[:], sz[:])
            P.inc(v.drain(), "sV")
            v.wait_ge(sems["sA"], aAOf(c, 11))
            v.tensor_scalar(sx[:], sxtl[:], srms[:, 1:2], None, ALU.mult)
            P.inc(v.drain(), "sV")

    def emit_pool(c):
        @block.gpsimd
        def _(g):
            g.wait_ge(sems["tX"], lb_x + (c + 1) * 16)
            if c >= 1:
                g.wait_ge(sems["sV"], vOf(c - 1, 59))
            dcol = c * 128
            g.tensor_mul(sxdt[:].rearrange("p (a b) -> p a b", a=128),
                         sx[:].rearrange("p (a b) -> p a b", a=128),
                         dt_bf[:, dcol:dcol + 128].unsqueeze(2)
                         .broadcast_to([128, 128, 64]))
            P.inc(g.drain(), "sG")
            if c >= 1:
                g.wait_ge(sems["sV"], vOf(c - 1, 60))
            g.tensor_mul(sxtl[:].rearrange("p (a b) -> p a b", a=128),
                         sxdt[:].rearrange("p (a b) -> p a b", a=128),
                         dexp_bf[:, dcol:dcol + 128].unsqueeze(2)
                         .broadcast_to([128, 128, 64]))
            P.inc(g.drain(), "sG")
            for gg in range(8):
                g.wait_ge(sems["sV"], vOf(c, 9 + 7 * gg))
                g.tensor_copy(R16[:, gg * 1024:(gg + 1) * 1024],
                              R[:, gg * 1024:(gg + 1) * 1024])
                P.inc(g.drain(), "sG")

    def emit_scalar(c):
        @block.scalar
        def _(s):
            for gg in range(8):
                s.wait_ge(sems["sV"], vOf(c, 3 + 7 * gg))
                s.activation(out=sd[:], in_=sd[:], func=AF.Exp)
                P.inc(s.drain(), "sA")
            s.wait_ge(sems["tZ"], lb_z + (c + 1) * 16)
            s.activation(out=sz[:], in_=sz[:], func=AF.Silu)
            P.inc(s.drain(), "sA")
            s.wait_ge(sems["sV"], vOf(c, 59))
            P.inc(s.activation(out=sx[:], in_=sxtl[:], func=AF.Square,
                               accum_out=srms[:, 0:1]), "sA")
            s.activation(out=srms[:, 1:2], in_=srms[:, 0:1], func=AF.Ln,
                         scale=1.0 / DI, bias=eps_col[:])
            s.activation(out=srms[:, 1:2], in_=srms[:, 1:2], func=AF.Exp,
                         scale=-0.5)
            P.inc(s.drain(), "sA")

    for c in range(NCH):
        emit_sync(c)
        emit_tensor(c)
        emit_vector(c)
        emit_pool(c)
        emit_scalar(c)
        if SSD_BARRIER:
            P.barrier()
    emit_final_store()


def phase_outproj(nc, P, block, YG_TM, WoutT, PROJ_FM, m_normw, bufs, ps):
    sems = P.sem
    ygf, wbuf, evb = bufs
    for half in range(2):
        base = P.snap()
        t0, t1 = half * 1024, (half + 1) * 1024

        @block.sync
        def _(sy):
            for kc in range(NWO):
                sy.dma_start(out=ygf[:, kc * 1024:(kc + 1) * 1024],
                             in_=YG_TM[t0:t1, kc * 128:(kc + 1) * 128],
                             transpose=True).then_inc(sems["tL"], 16)
        P.bump("tL", NWO * 16)

        @block.gpsimd
        def _(g):
            for ot in range(32):
                sl = (ot % 2) * DI
                wsem = sems["wL"] if ot % 2 == 0 else sems["wL2"]
                if ot >= 2:
                    g.wait_ge(sems["sT"], base["sT"] + (ot - 2) + 1)
                for qq in range(4):
                    g.dma_start(out=wbuf[:, sl + qq * 2048: sl + (qq + 1) * 2048]
                                .rearrange("p (kc o) -> p kc o", kc=16),
                                in_=WoutT[qq * 2048:(qq + 1) * 2048,
                                          ot * 128:(ot + 1) * 128]
                                .rearrange("(kc p) o -> p kc o", p=128)
                                ).then_inc(wsem, 16)
        wL0, wL20 = P.cnt["wL"], P.cnt["wL2"]
        P.bump("wL", 16 * 4 * 16)
        P.bump("wL2", 16 * 4 * 16)
        sS0, sS20 = P.cnt["sS"], P.cnt["sS2"]

        @block.vector
        def _(v):
            for ot in range(32):
                sl = (ot % 2) * DI
                psl = (ot % 2) * 1024
                v.wait_ge(sems["wL"] if ot % 2 == 0 else sems["wL2"],
                          (wL0 if ot % 2 == 0 else wL20) + (ot // 2 + 1) * 64)
                w3 = wbuf[:, sl:sl + DI].rearrange("p (kc o) -> p kc o", kc=NWO)
                v.tensor_mul(w3, w3,
                             m_normw[:].unsqueeze(2).broadcast_to([128, NWO, 128]))
                P.inc(v.drain(), "sV")
                v.wait_ge(sems["sT"], base["sT"] + ot + 1)
                if ot >= 2:
                    v.wait_ge(sems["sS"] if ot % 2 == 0 else sems["sS2"],
                              (sS0 if ot % 2 == 0 else sS20)
                              + ((ot - 2) // 2 + 1) * 16)
                P.inc(v.tensor_copy(evb[:, psl:psl + 1024],
                                    ps[:, psl:psl + 1024]), "sV")
        
        @block.tensor
        def _(t):
            t.wait_ge(sems["tL"], base["tL"] + NWO * 16)
            for ot in range(32):
                sl = (ot % 2) * DI
                psl = (ot % 2) * 1024
                t.wait_ge(sems["sV"], base["sV"] + 2 * ot + 1)
                if ot >= 2:
                    t.wait_ge(sems["sV"], base["sV"] + 2 * (ot - 2) + 2)
                last = None
                for kc in range(NWO):
                    lhsT = wbuf[:, sl + kc * 128: sl + (kc + 1) * 128]
                    for tt in range(2):
                        last = t.matmul(ps[:, psl + tt * 512: psl + (tt + 1) * 512],
                                        lhsT,
                                        ygf[:, kc * 1024 + tt * 512: kc * 1024 + (tt + 1) * 512],
                                        start=(kc == 0), stop=(kc == NWO - 1))
                P.inc(last, "sT")

        @block.scalar
        def _(s):
            for ot in range(32):
                psl = (ot % 2) * 1024
                s.wait_ge(sems["sV"], base["sV"] + 2 * ot + 2)
                s.dma_start(out=PROJ_FM[ot * 128:(ot + 1) * 128, t0:t1],
                            in_=evb[:, psl:psl + 1024]).then_inc(
                    sems["sS"] if ot % 2 == 0 else sems["sS2"], 16)
        P.bump("sS", 16 * 16)
        P.bump("sS2", 16 * 16)

        P.barrier()


def phase_ffn1(nc, P, block, XN_TM, W1T4, W3T4, a1T, a3T, b1T_in, b3T_in,
               H_FM, W2T2, W2B, bufs, psG, psU, psla):
    sems = P.sem
    xnf, w1b, w3b, sgb, hvb, a1s, a3s, la1, la3, b1o, b3o, pcf, pcb = bufs

    base = P.snap()

    @block.sync
    def _(sy):
        for kc in range(NKC):
            sy.dma_start(out=xnf[:, kc * L:(kc + 1) * L],
                         in_=XN_TM[:, kc * 128:(kc + 1) * 128],
                         transpose=True).then_inc(sems["tL"], 16)
    P.bump("tL", NKC * 16)

    @block.gpsimd
    def _(g):
        g.dma_start(out=a1s[:], in_=a1T).then_inc(sems["aL"], 16)
        g.dma_start(out=a3s[:], in_=a3T).then_inc(sems["aL"], 16)
    P.bump("aL", 2 * 16)

    @block.tensor
    def _(t):
        if not FFN1_PREPASS:
            return
        t.wait_ge(sems["tL"], P.cnt["tL"])
        t.wait_ge(sems["aL"], P.cnt["aL"])
        for which, asb in ((0, a1s), (1, a3s)):
            if which == 1:
                t.wait_ge(sems["sV"], base["sV"] + 1)
            last = None
            for kc in range(NKC):
                for q in range(4):
                    last = t.matmul(psla[0:2, q * 512:(q + 1) * 512],
                                    asb[:, kc * 2:(kc + 1) * 2],
                                    xnf[:, kc * L + q * 512: kc * L + (q + 1) * 512],
                                    start=(kc == 0), stop=(kc == NKC - 1))
            P.inc(last, "sT")

    @block.vector
    def _(v):
        if not FFN1_PREPASS:
            return
        v.wait_ge(sems["sT"], base["sT"] + 1)
        P.inc(v.tensor_scalar(la1[:], psla[0:2, :], 0.5, None, ALU.mult), "sV")
        v.wait_ge(sems["sT"], base["sT"] + 2)
        P.inc(v.tensor_scalar(la3[:], psla[0:2, :], 0.5, None, ALU.mult), "sV")

    P.barrier()
    base = P.snap()

    @block.gpsimd
    def _(g):
        for ot in range(FFN1_NOT):
            sl = (ot % 2) * 1024
            wsem = sems["wL"] if ot % 2 == 0 else sems["wL2"]
            if ot >= 2:
                g.wait_ge(sems["sT"], base["sT"] + 2 * (ot - 2) + 2)
            g.dma_start(out=w1b[:, sl:sl + 1024], in_=W1T4[ot]).then_inc(wsem, 16)
            g.dma_start(out=w3b[:, sl:sl + 1024], in_=W3T4[ot]).then_inc(wsem, 16)
            bsl = (ot % 2) * 128
            g.dma_start(out=b1o[0:2, bsl:bsl + 128],
                        in_=b1T_in[0:2, ot * 128:(ot + 1) * 128]).then_inc(wsem, 16)
            g.dma_start(out=b3o[0:2, bsl:bsl + 128],
                        in_=b3T_in[0:2, ot * 128:(ot + 1) * 128]).then_inc(wsem, 16)
            if ot % 2 == 1 and (ot - 1) // 2 < NKC_W2:
                kc2 = (ot - 1) // 2
                g.wait_ge(sems["pc"], base["pc"] + (kc2 + 1) * 16)
                if kc2 >= 1:
                    g.wait_ge(sems["pS"], base["pS"] + kc2 * 16)
                g.tensor_copy(pcb[:], pcf[:])
                P.inc(g.drain(), "pcC")
    wL0, wL20 = P.cnt["wL"], P.cnt["wL2"]
    P.bump("wL", ((FFN1_NOT + 1) // 2) * 64)
    P.bump("wL2", (FFN1_NOT // 2) * 64)

    @block.tensor
    def _(t):
        for ot in range(FFN1_NOT):
            sl = (ot % 2) * 1024
            t.wait_ge(sems["wL"] if ot % 2 == 0 else sems["wL2"],
                      (wL0 if ot % 2 == 0 else wL20) + (ot // 2 + 1) * 64)
            if ot >= 1:
                t.wait_ge(sems["sA"], base["sA"] + ot)
                t.wait_ge(sems["sV"], base["sV"] + ot)
            last = None
            for kc in range(NKC):
                lhsT = w1b[:, sl + (kc % 8) * 128: sl + (kc % 8 + 1) * 128]
                for q in range(4):
                    last = t.matmul(psG[:, q * 512:(q + 1) * 512], lhsT,
                                    xnf[:, kc * L + q * 512: kc * L + (q + 1) * 512],
                                    start=(kc == 0), stop=False)
            bsl = (ot % 2) * 128
            if FFN1_LORA:
                for q in range(4):
                    last = t.matmul(psG[:, q * 512:(q + 1) * 512],
                                    b1o[0:2, bsl:bsl + 128],
                                    la1[0:2, q * 512:(q + 1) * 512],
                                    start=False, stop=True)
            else:
                for q in range(4):
                    last = t.matmul(psG[:, q * 512:(q + 1) * 512],
                                    w1b[:, sl:sl + 128],
                                    xnf[:, q * 512:(q + 1) * 512],
                                    start=False, stop=True)
            P.inc(last, "sT")
            last = None
            for kc in range(NKC):
                lhsT = w3b[:, sl + (kc % 8) * 128: sl + (kc % 8 + 1) * 128]
                for q in range(4):
                    last = t.matmul(psU[:, q * 512:(q + 1) * 512], lhsT,
                                    xnf[:, kc * L + q * 512: kc * L + (q + 1) * 512],
                                    start=(kc == 0), stop=False)
            if FFN1_LORA:
                for q in range(4):
                    last = t.matmul(psU[:, q * 512:(q + 1) * 512],
                                    b3o[0:2, bsl:bsl + 128],
                                    la3[0:2, q * 512:(q + 1) * 512],
                                    start=False, stop=True)
            else:
                for q in range(4):
                    last = t.matmul(psU[:, q * 512:(q + 1) * 512],
                                    w3b[:, sl:sl + 128],
                                    xnf[:, q * 512:(q + 1) * 512],
                                    start=False, stop=True)
            P.inc(last, "sT")

    @block.scalar
    def _(s):
        for ot in range(FFN1_NOT):
            gsl = (ot % 2) * L
            s.wait_ge(sems["sT"], base["sT"] + 2 * ot + 1)
            if ot >= 2:
                s.wait_ge(sems["sV"], base["sV"] + ot - 1)
            P.inc(s.activation(out=sgb[:, gsl:gsl + L], in_=psG[:], func=AF.Silu),
                  "sA")
            if ot % 2 == 1 and ot >= 3 and (ot - 3) // 2 < NKC_W2 - 1:
                kc2 = (ot - 3) // 2
                s.wait_ge(sems["pcC"], base["pcC"] + kc2 + 1)
                s.dma_start(out=W2B[kc2], in_=pcb[:]).then_inc(sems["pS"], 16)
        s.wait_ge(sems["pcC"], base["pcC"] + NKC_W2)
        s.dma_start(out=W2B[NKC_W2 - 1], in_=pcb[:]).then_inc(sems["pS"], 16)
    P.bump("pS", NKC_W2 * 16)
    hS0, hS20 = P.cnt["sS"], P.cnt["sS2"]

    @block.vector
    def _(v):
        for ot in range(FFN1_NOT):
            gsl = (ot % 2) * L
            v.wait_ge(sems["sT"], base["sT"] + 2 * ot + 2)
            v.wait_ge(sems["sA"], base["sA"] + ot + 1)
            if ot >= 2:
                v.wait_ge(sems["sS"] if ot % 2 == 0 else sems["sS2"],
                          (hS0 if ot % 2 == 0 else hS20)
                          + ((ot - 2) // 2 + 1) * 16)
            P.inc(v.tensor_mul(hvb[:, gsl:gsl + L], sgb[:, gsl:gsl + L], psU[:]),
                  "sV")

    @block.sync
    def _(sy):
        for ot in range(FFN1_NOT):
            gsl = (ot % 2) * L
            sy.wait_ge(sems["sV"], base["sV"] + ot + 1)
            sy.dma_start(out=H_FM[ot * 128:(ot + 1) * 128, :],
                         in_=hvb[:, gsl:gsl + L]).then_inc(
                sems["sS"] if ot % 2 == 0 else sems["sS2"], 16)
            if ot % 2 == 0 and ot // 2 < NKC_W2:
                kc2 = ot // 2
                if kc2 >= 1:
                    sy.wait_ge(sems["pcC"], base["pcC"] + kc2)
                sy.dma_start(out=pcf[:], in_=W2T2[kc2 * 128:(kc2 + 1) * 128, :]
                             ).then_inc(sems["pc"], 16)
    P.bump("sS", ((FFN1_NOT + 1) // 2) * 16)
    P.bump("sS2", (FFN1_NOT // 2) * 16)
    P.bump("pc", NKC_W2 * 16)


def phase_ffn2(nc, P, block, H_FM, W2B, a2T, b2T_in, G_FM, G_LO, bufs, ps, psla):
    sems = P.sem
    hb, w2b, evb, evlo, a2s, la2, b2s = bufs

    @block.gpsimd
    def _(g):
        g.dma_start(out=a2s[:], in_=a2T).then_inc(sems["aL"], 16)
        g.dma_start(out=b2s[:], in_=b2T_in).then_inc(sems["aL"], 16)
    P.bump("aL", 2 * 16)

    for sl4 in range(4):
        base = P.snap()
        t0 = sl4 * 512

        @block.sync
        def _(sy):
            for kc in range(NKC_H):
                sy.dma_start(out=hb[:, kc * 512:(kc + 1) * 512],
                             in_=H_FM[kc * 128:(kc + 1) * 128, t0:t0 + 512]
                             ).then_inc(sems["tL"], 16)
        P.bump("tL", NKC_H * 16)

        @block.scalar
        def _(s):
            for ot in range(32):
                wsl = (ot % 2) * (NKC_W2 * 128)
                wsem = sems["wL"] if ot % 2 == 0 else sems["wL2"]
                if ot >= 2:
                    s.wait_ge(sems["sT"], base["sT"] + 1 + (ot - 2) + 1)
                for qq in range(4):
                    n0 = qq * 11
                    n1 = min(NKC_W2, (qq + 1) * 11)
                    s.dma_start(out=w2b[:, wsl + n0 * 128: wsl + n1 * 128]
                                .rearrange("p (kc o) -> p kc o", o=128),
                                in_=W2B[n0:n1, :, ot * 128:(ot + 1) * 128]
                                .rearrange("kc p o -> p kc o")
                                ).then_inc(wsem, 16)
        wL0, wL20 = P.cnt["wL"], P.cnt["wL2"]
        P.bump("wL", 16 * 4 * 16)
        P.bump("wL2", 16 * 4 * 16)
        eS0, eS20 = P.cnt["sS"], P.cnt["sS2"]

        @block.tensor
        def _(t):
            t.wait_ge(sems["tL"], base["tL"] + NKC_H * 16)
            t.wait_ge(sems["aL"], P.cnt["aL"])
            last = None
            for kc in range(NKC_H):
                last = t.matmul(psla[0:2, 0:512], a2s[:, kc * 2:(kc + 1) * 2],
                                hb[:, kc * 512:(kc + 1) * 512],
                                start=(kc == 0), stop=(kc == NKC_H - 1))
            P.inc(last, "sT")

        @block.vector
        def _(v):
            v.wait_ge(sems["sT"], base["sT"] + 1)
            P.inc(v.tensor_scalar(la2[:], psla[0:2, 0:512], 0.5, None, ALU.mult),
                  "sV")

        @block.tensor
        def _(t):
            t.wait_ge(sems["sV"], base["sV"] + 1)
            for ot in range(32):
                wsl = (ot % 2) * (NKC_W2 * 128)
                psl = (ot % 2) * 512
                t.wait_ge(sems["wL"] if ot % 2 == 0 else sems["wL2"],
                          (wL0 if ot % 2 == 0 else wL20) + (ot // 2 + 1) * 64)
                if ot >= 2:
                    t.wait_ge(sems["sV"], base["sV"] + 1 + (ot - 2) + 1)
                last = None
                for kc in range(NKC_H):
                    lhsT = w2b[:, wsl + (kc % NKC_W2) * 128:
                               wsl + (kc % NKC_W2 + 1) * 128]
                    last = t.matmul(ps[:, psl:psl + 512], lhsT,
                                    hb[:, kc * 512:(kc + 1) * 512],
                                    start=(kc == 0), stop=False)
                last = t.matmul(ps[:, psl:psl + 512],
                                b2s[0:2, ot * 128:(ot + 1) * 128], la2[:],
                                start=False, stop=True)
                P.inc(last, "sT")

        @block.vector
        def _(v):
            for ot in range(32):
                psl = (ot % 2) * 512
                esl = (ot % 2) * 512
                v.wait_ge(sems["sT"], base["sT"] + 1 + ot + 1)
                if ot >= 2:
                    v.wait_ge(sems["sS"] if ot % 2 == 0 else sems["sS2"],
                              (eS0 if ot % 2 == 0 else eS20)
                              + ((ot - 2) // 2 + 1) * 32)
                v.tensor_copy(evb[:, esl:esl + 512], ps[:, psl:psl + 512])
                v.drain()
                P.inc(v.tensor_sub(evlo[:, esl:esl + 512], ps[:, psl:psl + 512],
                                   evb[:, esl:esl + 512]), "sV")

        @block.gpsimd
        def _(g):
            for ot in range(32):
                esl = (ot % 2) * 512
                g.wait_ge(sems["sV"], base["sV"] + 1 + ot + 1)
                gsem = sems["sS"] if ot % 2 == 0 else sems["sS2"]
                g.dma_start(out=G_FM[ot * 128:(ot + 1) * 128, t0:t0 + 512],
                            in_=evb[:, esl:esl + 512]).then_inc(gsem, 16)
                g.dma_start(out=G_LO[ot * 128:(ot + 1) * 128, t0:t0 + 512],
                            in_=evlo[:, esl:esl + 512]).then_inc(gsem, 16)
        P.bump("sS", 16 * 32)
        P.bump("sS2", 16 * 32)

        P.barrier()


def phase_final(nc, P, block, x_in, XN_TM, PROJ_FM, G_FM, G_LO, out, bufs):
    sems = P.sem
    xb, xnb, pjb, ghb, glb, gf, sf, ob = bufs
    base = P.snap()
    fS0, fS20 = P.cnt["sS"], P.cnt["sS2"]

    @block.sync
    def _(sy):
        for i in range(NT):
            sl = (i % 2) * DM
            asem = sems["aL"] if i % 2 == 0 else sems["aL2"]
            tsem = sems["tL"] if i % 2 == 0 else sems["tL2"]
            if i >= 2:
                sy.wait_ge(sems["sV"], base["sV"] + (i - 2) * 4 + 4)
            sy.dma_start(out=xb[:, sl:sl + DM],
                         in_=x_in[i * 128:(i + 1) * 128, :]).then_inc(asem, 16)
            sy.dma_start(out=pjb[:, sl:sl + DM],
                         in_=PROJ_FM[:, i * 128:(i + 1) * 128],
                         transpose=True).then_inc(tsem, 16)
            sy.dma_start(out=ghb[:, sl:sl + DM],
                         in_=G_FM[:, i * 128:(i + 1) * 128],
                         transpose=True).then_inc(tsem, 16)
            sy.dma_start(out=glb[:, sl:sl + DM],
                         in_=G_LO[:, i * 128:(i + 1) * 128],
                         transpose=True).then_inc(tsem, 16)
    aL0, aL20 = P.cnt["aL"], P.cnt["aL2"]
    tL0, tL20 = P.cnt["tL"], P.cnt["tL2"]
    P.bump("aL", (NT // 2) * 16)
    P.bump("aL2", (NT // 2) * 16)
    P.bump("tL", (NT // 2) * 48)
    P.bump("tL2", (NT // 2) * 48)

    @block.scalar
    def _(s):
        for i in range(NT):
            sl = (i % 2) * DM
            if i >= 2:
                s.wait_ge(sems["sV"], base["sV"] + (i - 2) * 4 + 2)
            s.dma_start(out=xnb[:, sl:sl + DM],
                        in_=XN_TM[i * 128:(i + 1) * 128, :]).then_inc(
                sems["pc"] if i % 2 == 0 else sems["pc2"], 16)
            s.wait_ge(sems["sV"], base["sV"] + i * 4 + 1)
            P.inc(s.activation(out=gf[:, sl:sl + DM], in_=gf[:, sl:sl + DM],
                               func=AF.Silu), "sA")
    pc0, pc20 = P.cnt["pc"], P.cnt["pc2"]
    P.bump("pc", (NT // 2) * 16)
    P.bump("pc2", (NT // 2) * 16)

    @block.vector
    def _(v):
        for i in range(NT):
            sl = (i % 2) * DM
            v.wait_ge(sems["tL"] if i % 2 == 0 else sems["tL2"],
                      (tL0 if i % 2 == 0 else tL20) + (i // 2 + 1) * 48)
            P.inc(v.tensor_add(gf[:, sl:sl + DM], ghb[:, sl:sl + DM],
                               glb[:, sl:sl + DM]), "sV")
            v.wait_ge(sems["pc"] if i % 2 == 0 else sems["pc2"],
                      (pc0 if i % 2 == 0 else pc20) + (i // 2 + 1) * 16)
            P.inc(v.tensor_add(sf[:, sl:sl + DM], xnb[:, sl:sl + DM],
                               pjb[:, sl:sl + DM]), "sV")
            v.drain()
            v.wait_ge(sems["sA"], base["sA"] + i + 1)
            P.inc(v.tensor_mul(sf[:, sl:sl + DM], sf[:, sl:sl + DM],
                               gf[:, sl:sl + DM]), "sV")
            v.drain()
            v.wait_ge(sems["aL"] if i % 2 == 0 else sems["aL2"],
                      (aL0 if i % 2 == 0 else aL20) + (i // 2 + 1) * 16)
            if i >= 2:
                v.wait_ge(sems["sS"] if i % 2 == 0 else sems["sS2"],
                          (fS0 if i % 2 == 0 else fS20) + ((i - 2) // 2 + 1) * 16)
            P.inc(v.tensor_add(ob[:, sl:sl + DM], xb[:, sl:sl + DM],
                               sf[:, sl:sl + DM]), "sV")

    @block.gpsimd
    def _(g):
        for i in range(NT):
            sl = (i % 2) * DM
            g.wait_ge(sems["sV"], base["sV"] + i * 4 + 4)
            g.dma_start(out=out[i * 128:(i + 1) * 128, :],
                        in_=ob[:, sl:sl + DM]).then_inc(
                sems["sS"] if i % 2 == 0 else sems["sS2"], 16)
    P.bump("sS", (NT // 2) * 16)
    P.bump("sS2", (NT // 2) * 16)


class _Scope:

    def __init__(self, nc):
        from contextlib import ExitStack
        self.nc = nc
        self.ctx = ExitStack()

    def sb(self, name, shape, dtype):
        return self.ctx.enter_context(self.nc.sbuf_tensor(name, shape, dtype))

    def ps(self, name, shape, dtype):
        return self.ctx.enter_context(self.nc.psum_tensor(name, shape, dtype))

    def close(self):
        self.ctx.close()


def build(taps=(), upto=None):
    nc = bass.Bass("TRN2", target_bir_lowering=False)

    def din(name, shape, dtype=F32):
        return nc.dram_tensor(name, shape, dtype, kind="ExternalInput").ap()

    x_in = din("x", [L, DM])
    WinT4 = din("WinT4", [NOT_IN, 128, NKC, 128])
    conv_w = din("conv_w", [128, NCT * CONVK])
    conv_b = din("conv_b", [128, NCT])
    dt_bias = din("dt_bias", [128, 1])
    A_log = din("A_log", [128, 1])
    Drep = din("Drep", [128, NXT])
    normw = din("normw", [128, NWO])
    WoutT = din("WoutT", [DI, DM])
    W1T4 = din("W1T4", [NOT_F1, 128, 8, 128])
    W3T4 = din("W3T4", [NOT_F1, 128, 8, 128])
    W2T2 = din("W2T2", [W2P, DM])
    a1T = din("a1T", [128, NKC * 2])
    b1T = din("b1T", [2, HF])
    a3T = din("a3T", [128, NKC * 2])
    b3T = din("b3T", [2, HF])
    a2T = din("a2T", [128, NKC_H * 2])
    b2T = din("b2T", [2, DM])

    out = nc.dram_tensor("out", [L, DM], F32, kind="ExternalOutput").ap()

    def scr(name, shape, dtype):
        kind = "ExternalOutput" if name in taps else "Internal"
        return nc.dram_tensor(name, shape, dtype, kind=kind).ap()

    XN_TM = scr("XN_TM", [L, DM], BF16)
    ZX_FM = scr("ZX_FM", [DIN, L], BF16)
    XC_FM = scr("XC_FM", [DI, L], BF16)
    XD_FM = scr("XD_FM", [DI, L], BF16)
    BC_FM = scr("BC_FM", [2 * NG * DS, L], BF16)
    CS_HI = scr("CS_HI", [128, L], BF16)
    CS_LO = scr("CS_LO", [128, L], BF16)
    YG_TM = scr("YG_TM", [L, DI], BF16)
    PROJ_FM = scr("PROJ_FM", [DM, L], BF16)
    H_FM = scr("H_FM", [HF, L], BF16)
    G_FM = scr("G_FM", [DM, L], BF16)
    G_LO = scr("G_LO", [DM, L], BF16)
    W2B = scr("W2B", [NKC_W2, 128, DM], BF16)

    from contextlib import ExitStack
    with ExitStack() as ctx:
        def sb(name, shape, dtype):
            return ctx.enter_context(nc.sbuf_tensor(name, shape, dtype))

        ident_bf = sb("ident_bf", [128, 128], BF16)
        ident_f32 = sb("ident_f32", [128, 128], F32)
        ones_bf = sb("ones_bf", [1, 512], BF16)
        ones_f32 = sb("ones_f32", [1, 512], F32)
        zeros128 = sb("zeros128", [128, 128], F32)
        m0mask = sb("m0mask", [128, 128], F32)
        vec_dtb = sb("vec_dtb", [128, 1], F32)
        vec_nA = sb("vec_nA", [128, 1], F32)
        m_convb = sb("m_convb", [128, NCT], F32)
        m_convw = sb("m_convw", [128, NCT * CONVK], F32)
        m_Drep = sb("m_Drep", [128, NXT], F32)
        m_normw = sb("m_normw", [128, NWO], F32)
        eps_col = sb("eps_col", [128, 1], F32)
        one_col = sb("one_col", [128, 1], F32)

        sems = {n: ctx.enter_context(nc.semaphore(n)) for n in SEM_NAMES}
        block = ctx.enter_context(nc.Block())
        P = Prog(nc, block, sems)

        phase_const(nc, P, block,
                    (ident_bf, ident_f32, ones_bf, ones_f32, zeros128, m0mask,
                     vec_dtb, vec_nA, m_convb, m_convw, m_Drep, m_normw,
                     eps_col, one_col),
                    (dt_bias, A_log, conv_b, conv_w, Drep, normw))
        P.barrier()

        sc = _Scope(nc)
        ln_x = sc.sb("ln_x", [128, 2 * DM], F32)
        ln_xn = sc.sb("ln_xn", [128, DM], BF16)
        ln_st = sc.sb("ln_st", [128, 8], F32)
        ip_xnf = sc.sb("ip_xnf", [128, NKC * L], BF16)
        ip_w = sc.sb("ip_w", [128, 2 * DM], BF16)
        ip_ev = sc.sb("ip_ev", [128, 2 * L], BF16)
        ip_ps = sc.ps("ip_ps", [128, 2 * L], F32)
        xnf_ready = phase_ln(nc, P, block, x_in, XN_TM, eps_col, ident_bf,
                             (ln_x, ln_xn, ln_st, ip_xnf, ip_ev), ip_ps)
        phase_inproj(nc, P, block, WinT4, ZX_FM, (ip_xnf, ip_w, ip_ev), ip_ps,
                     xnf_ready)
        P.barrier()
        sc.close()

        sc = _Scope(nc)
        cv_xt = sc.sb("cv_xt", [128, 4112], BF16)
        cv_dg = sc.sb("cv_dg", [128, 1024], BF16)
        cv_out = sc.sb("cv_out", [128, 2 * L], BF16)
        cv_xd = sc.sb("cv_xd", [128, 2 * L], BF16)
        cv_ps = sc.ps("cv_ps", [128, 2 * L], F32)
        phase_conv(nc, P, block, ZX_FM, XC_FM, XD_FM, BC_FM, m_convw,
                   m_convb, m_Drep, ident_bf, (cv_xt, cv_dg, cv_out, cv_xd),
                   cv_ps)
        P.barrier()
        sc.close()

        rs = _Scope(nc)
        r_dt_tm = rs.sb("r_dt_tm", [128, L], BF16)
        r_expcs_tm = rs.sb("r_expcs_tm", [128, L], BF16)
        r_dexp_tm = rs.sb("r_dexp_tm", [128, L], BF16)
        r_expTrep = rs.sb("r_expTrep", [128, L], F32)
        r_ncs_hi = rs.sb("r_ncs_hi", [128, L], BF16)
        r_ncs_lo = rs.sb("r_ncs_lo", [128, L], BF16)
        r_R = rs.sb("r_R", [128, DI], F32)
        r_R16 = rs.sb("r_R16", [128, DI], BF16)

        sc = _Scope(nc)
        dtp_raw = sc.sb("dtp_raw", [128, L], BF16)
        dtp_sp1 = sc.sb("dtp_sp1", [128, L], F32)
        dtp_sp2 = sc.sb("dtp_sp2", [128, L], F32)
        dtp_cs = sc.sb("dtp_cs", [128, L], F32)
        dtp_dexp = sc.sb("dtp_dexp", [128, L], F32)
        dtp_cshi = sc.sb("dtp_cshi", [128, L], BF16)
        dtp_cslo = sc.sb("dtp_cslo", [128, L], BF16)
        dtp_trow = sc.sb("dtp_trow", [1, 128], F32)
        dtp_psb = sc.ps("dtp_psb", [128, 2048], F32)
        dtp_ps2 = sc.ps("dtp_ps2", [128, 1024], F32)
        phase_dt(nc, P, block, ZX_FM, CS_HI, CS_LO,
                 (vec_dtb, vec_nA, one_col, ident_f32, ones_f32, zeros128),
                 (r_dt_tm, r_expcs_tm, r_dexp_tm, r_expTrep, r_ncs_hi,
                  r_ncs_lo),
                 (dtp_raw, dtp_sp1, dtp_sp2, dtp_cs, dtp_dexp, dtp_cshi,
                  dtp_cslo, dtp_trow),
                 dtp_psb, dtp_ps2)
        P.barrier()
        sc.close()

        sc = _Scope(nc)
        s_x = sc.sb("s_x", [128, DI], BF16)
        s_xd = sc.sb("s_xd", [128, DI], BF16)
        s_z = sc.sb("s_z", [128, DI], BF16)
        s_xdt = sc.sb("s_xdt", [128, DI], BF16)
        s_xtl = sc.sb("s_xtl", [128, DI], BF16)
        s_m = sc.sb("s_m", [128, 2048], BF16)
        s_d = sc.sb("s_d", [128, 2048], F32)
        s_s = sc.sb("s_s", [128, 1024], BF16)
        s_btm = sc.sb("s_btm", [128, 1024], BF16)
        s_bcf = sc.sb("s_bcf", [128, 2048], BF16)
        s_tmp = sc.sb("s_tmp", [128, 1024], F32)
        s_fh = sc.sb("s_fh", [1, 4096], BF16)
        s_fl = sc.sb("s_fl", [1, 4096], BF16)
        s_rms = sc.sb("s_rms", [128, 2], F32)
        s_psA = sc.ps("s_psA", [128, 3584], F32)
        s_psT = sc.ps("s_psT", [128, 1024], BF16)
        phase_ssd(nc, P, block, XC_FM, XD_FM, ZX_FM, BC_FM, CS_HI,
                  CS_LO, YG_TM,
                  (ident_bf, ones_bf, eps_col, m0mask),
                  (r_dt_tm, r_expcs_tm, r_dexp_tm, r_expTrep, r_ncs_hi,
                   r_ncs_lo, r_R, r_R16),
                  (s_x, s_xd, s_z, s_xdt, s_xtl, s_m, s_d, s_s, s_btm,
                   s_bcf, s_tmp, s_fh, s_fl, s_rms),
                  s_psA, s_psT)
        P.barrier()
        sc.close()
        rs.close()

        sc = _Scope(nc)
        op_ygf = sc.sb("op_ygf", [128, NWO * 1024], BF16)
        op_w = sc.sb("op_w", [128, 2 * DI], BF16)
        op_ev = sc.sb("op_ev", [128, 2048], BF16)
        op_ps = sc.ps("op_ps", [128, 2048], F32)
        if upto == "ssd":
            sc.close()
            return nc
        phase_outproj(nc, P, block, YG_TM, WoutT, PROJ_FM, m_normw,
                      (op_ygf, op_w, op_ev), op_ps)
        sc.close()
        if upto == "outproj":
            return nc

        sc = _Scope(nc)
        f1_xnf = sc.sb("f1_xnf", [128, NKC * L], BF16)
        f1_w1 = sc.sb("f1_w1", [128, 2048], BF16)
        f1_w3 = sc.sb("f1_w3", [128, 2048], BF16)
        f1_sg = sc.sb("f1_sg", [128, 2 * L], BF16)
        f1_hv = sc.sb("f1_hv", [128, 2 * L], BF16)
        f1_a1 = sc.sb("f1_a1", [128, NKC * 2], BF16)
        f1_a3 = sc.sb("f1_a3", [128, NKC * 2], BF16)
        f1_la1 = sc.sb("f1_la1", [2, L], BF16)
        f1_la3 = sc.sb("f1_la3", [2, L], BF16)
        f1_b1 = sc.sb("f1_b1", [2, 256], BF16)
        f1_b3 = sc.sb("f1_b3", [2, 256], BF16)
        f1_psG = sc.ps("f1_psG", [128, 2048], F32)
        f1_psU = sc.ps("f1_psU", [128, 2048], F32)
        f1_pcf = sc.sb("f1_pcf", [128, DM], F32)
        f1_pcb = sc.sb("f1_pcb", [128, DM], BF16)
        phase_ffn1(nc, P, block, XN_TM, W1T4, W3T4, a1T, a3T, b1T, b3T, H_FM,
                   W2T2, W2B,
                   (f1_xnf, f1_w1, f1_w3, f1_sg, f1_hv, f1_a1, f1_a3, f1_la1,
                    f1_la3, f1_b1, f1_b3, f1_pcf, f1_pcb), f1_psG, f1_psU,
                   f1_psG)
        P.barrier()
        sc.close()
        if upto == "ffn1":
            return nc

        sc = _Scope(nc)
        f2_h = sc.sb("f2_h", [128, NKC_H * 512], BF16)
        f2_w = sc.sb("f2_w", [128, 2 * NKC_W2 * 128], BF16)
        f2_ev = sc.sb("f2_ev", [128, 1024], BF16)
        f2_evlo = sc.sb("f2_evlo", [128, 1024], BF16)
        f2_a2 = sc.sb("f2_a2", [128, NKC_H * 2], BF16)
        f2_la2 = sc.sb("f2_la2", [2, 512], BF16)
        f2_b2 = sc.sb("f2_b2", [2, DM], BF16)
        f2_ps = sc.ps("f2_ps", [128, 1024], F32)
        f2_psla = sc.ps("f2_psla", [128, 512], F32)
        phase_ffn2(nc, P, block, H_FM, W2B, a2T, b2T, G_FM, G_LO,
                   (f2_h, f2_w, f2_ev, f2_evlo, f2_a2, f2_la2, f2_b2), f2_ps,
                   f2_psla)
        sc.close()
        if upto == "ffn2":
            return nc

        sc = _Scope(nc)
        fn_x = sc.sb("fn_x", [128, 2 * DM], F32)
        fn_xn = sc.sb("fn_xn", [128, 2 * DM], BF16)
        fn_pj = sc.sb("fn_pj", [128, 2 * DM], BF16)
        fn_gh = sc.sb("fn_gh", [128, 2 * DM], BF16)
        fn_gl = sc.sb("fn_gl", [128, 2 * DM], BF16)
        fn_gf = sc.sb("fn_gf", [128, 2 * DM], F32)
        fn_sf = sc.sb("fn_sf", [128, 2 * DM], F32)
        fn_o = sc.sb("fn_o", [128, 2 * DM], F32)
        phase_final(nc, P, block, x_in, XN_TM, PROJ_FM, G_FM, G_LO, out,
                    (fn_x, fn_xn, fn_pj, fn_gh, fn_gl, fn_gf, fn_sf, fn_o))
        P.barrier()
        sc.close()

    return nc


def kernel(**inp):
    w = prep_weights(inp)
    x = inp["L0_attn_hidden_states"]
    nc = build(taps=TAPS)
    in_maps = []
    for c in range(8):
        m = {"x": _t(x[c])}
        m.update(w)
        in_maps.append(m)
    res = run_bass_kernel_spmd(nc, in_maps, core_ids=list(range(8)))
    out = np.stack([np.asarray(r["out"], dtype=np.float32) for r in res.results])
    if TAPS:
        kernel.tap_results = res.results
    return out
